# version 08
# baseline (speedup 1.0000x reference)
import math
import contextlib
import numpy as np

import concourse.bass as bass
import concourse.mybir as mybir

F32 = mybir.dt.float32
BF16 = mybir.dt.bfloat16
I32 = mybir.dt.int32
AF = mybir.ActivationFunctionType
ALU = mybir.AluOpType
AX = mybir.AxisListType

D = 2048
NT = 1024
W = 512
NH = 16
DH = 128
NP = 16
KT = NP + W
NC_F = D // 128
HE = 1024
NE = 8
NEG = -np.float32(3.4028235e38)

ENGS = ("pe", "dve", "act", "pool", "sync")
PROMOTE = ("d_init", "d_initp", "d_init2", "d_initp2", "d_init3", "d_initp3")


class Tile:
    __slots__ = ("t", "name", "last_w", "readers", "pending", "dead", "dsem")

    def __init__(self, t, name):
        self.t = t
        self.name = name
        self.last_w = None
        self.readers = []
        self.pending = []
        self.dead = False
        self.dsem = None

    def __getitem__(self, idx):
        return self.t[idx]

    @property
    def ap(self):
        import bass_rust
        if isinstance(self.t, bass_rust.AP):
            return self.t
        return self.t[:]


class Builder:
    def __init__(self, nc):
        self.nc = nc
        self.ops = {e: [] for e in ENGS}
        import collections
        self.cnt = collections.defaultdict(int)
        for s in ("pe", "dve", "act", "pool"):
            self.cnt[s] = 0
        self.nops = 0
        self.init_sem = {"sync": "d_init", "pool": "d_initp"}
        self.barrier_pending = {}

    def set_init_phase(self, n):
        self.init_sem = {"sync": f"d_init{n}", "pool": f"d_initp{n}"}

    def barrier(self):
        snap = [(s, v) for s, v in self.cnt.items() if v > 0]
        for e in ENGS:
            self.barrier_pending[e] = list(snap)

    def _take_barrier(self, eng):
        w = self.barrier_pending.pop(eng, [])
        return w

    def op(self, eng, fn, reads=(), writes=(), extra=(), dma=False):
        waits = self._take_barrier(eng) + list(extra)
        for t in reads:
            assert not t.dead, t.name
            if t.last_w is not None:
                waits.append(t.last_w)
        for t in writes:
            assert not t.dead, t.name
            if t.last_w is not None:
                waits.append(t.last_w)
            waits.extend(t.readers)
        if dma:
            sem = dma if isinstance(dma, str) else self.init_sem[eng]
            inc = 16
        else:
            sem, inc = eng, 1
        self.cnt[sem] += inc
        tok = (sem, self.cnt[sem])
        self.ops[eng].append((fn, tuple(waits), (sem, inc)))
        self.nops += 1
        for t in reads:
            t.readers.append(tok)
        for t in writes:
            t.last_w = tok
            t.readers = [tok]
        return tok

    def mm(self, bank, out_ap, lh, rh, reads=(), start=False, stop=False,
           token=False, extra=()):
        waits = self._take_barrier("pe") + list(extra)
        if start:
            if bank.last_w is not None:
                waits.append(bank.last_w)
            waits.extend(bank.readers)
            bank.pending = []
        for t in reads:
            assert not t.dead, t.name
            if t.last_w is not None:
                waits.append(t.last_w)
            bank.pending.append(t)
        tok, inc = None, None
        if stop or token:
            self.cnt["pe"] += 1
            tok = ("pe", self.cnt["pe"])
            inc = ("pe", 1)
        self.ops["pe"].append((
            (lambda o=out_ap, a=lh, b=rh, s=start, p=stop: (
                lambda t: t.matmul(o, a, b, start=s, stop=p)))(),
            tuple(waits), inc))
        self.nops += 1
        if tok is not None:
            for t in bank.pending:
                t.readers.append(tok)
            bank.pending = []
        if stop:
            bank.last_w = tok
            bank.readers = [tok]
        return tok

    def mmg(self, bank, mms, reads=()):
        n = len(mms)
        tok = None
        for i, (o, lh, rh) in enumerate(mms):
            tok = self.mm(bank, o, lh, rh, reads=reads if i == 0 else (),
                          start=(i == 0), stop=(i == n - 1))
        return tok

    def dma(self, eng, out_ap, in_ap, reads=(), writes=(), extra=()):
        sem = True
        for t in writes:
            if t.dsem:
                sem = t.dsem
        return self.op(eng, lambda e: e.dma_start(out=out_ap, in_=in_ap),
                       reads=reads, writes=writes, extra=extra, dma=sem)

    def dma_out(self, out_ap, in_ap, reads=()):
        return self.op("sync", lambda e: e.dma_start(out=out_ap, in_=in_ap),
                       reads=reads, dma="d_out")

    def finalize(self):
        nc = self.nc
        finals = dict(self.cnt)
        with contextlib.ExitStack() as st:
            sems = {s: st.enter_context(nc.semaphore(s))
                    for s in self.cnt if self.cnt[s] > 0}
            blk = st.enter_context(nc.Block())

            def emit(eng_obj, stream, last_wait=None):
                done = {}
                for fn, waits, inc in stream:
                    byname = {}
                    for (s, v) in waits:
                        if s in PROMOTE:
                            v = finals[s]
                        if done.get(s, 0) < v and byname.get(s, 0) < v:
                            byname[s] = v
                    for s, v in sorted(byname.items()):
                        eng_obj.wait_ge(sems[s], v)
                        done[s] = v
                    inst = fn(eng_obj)
                    if inc is not None:
                        s, amt = inc
                        inst.then_inc(sems[s], amt)
                if last_wait is not None and finals.get(last_wait, 0) > 0:
                    eng_obj.wait_ge(sems[last_wait], finals[last_wait])

            @blk.tensor
            def _(t):
                emit(t, self.ops["pe"])

            @blk.vector
            def _(v):
                emit(v, self.ops["dve"])

            @blk.scalar
            def _(s):
                emit(s, self.ops["act"])

            @blk.gpsimd
            def _(g):
                emit(g, self.ops["pool"])

            @blk.sync
            def _(s):
                emit(s, self.ops["sync"], last_wait="d_out")


def host_inputs(x_full, wd, core, shared=None):
    import ml_dtypes
    c = core
    if shared is not None and shared:
        m = dict(shared)
        xt = np.ascontiguousarray(
            np.asarray(x_full).reshape(8192, D)[c * NT:(c + 1) * NT, :].T)
        rows = np.arange(c * NT, (c + 1) * NT)
        pos = (rows % 4096).astype(np.float32)
        m["xT"] = xt
        m["posB"] = np.ascontiguousarray(
            np.broadcast_to(pos[None, :], (128, NT)))
        return m
    xt = np.ascontiguousarray(
        np.asarray(x_full).reshape(8192, D)[c * NT:(c + 1) * NT, :].T)
    rows = np.arange(c * NT, (c + 1) * NT)
    pos = (rows % 4096).astype(np.float32)
    posB = np.ascontiguousarray(np.broadcast_to(pos[None, :], (128, NT)))
    p = np.arange(128)
    k2col = (2 * (p // 2)).astype(np.float32)[:, None]
    s4 = float(DH) ** -0.25
    altcol = (np.where(p % 2 == 1, 1.0, -1.0) * s4).astype(np.float32)[:, None]
    ident = np.eye(128, dtype=ml_dtypes.bfloat16)
    identf = np.eye(8, dtype=np.float32)
    ones2 = np.ones((2, 128), dtype=ml_dtypes.bfloat16)
    sel8 = np.zeros((8, 8 * 128), dtype=ml_dtypes.bfloat16)
    for e in range(8):
        sel8[e, e * 128:(e + 1) * 128] = 1.0
    qt = np.arange(W)[:, None]
    ktw = np.arange(W)[None, :]
    allowed_w = (ktw < 16) | ((qt >= 16) & (qt >= ktw))
    maskqt = np.where(allowed_w, 0.0, NEG).astype(np.float32)
    maskqt = np.ascontiguousarray(maskqt.reshape(4, 128, W))
    m01 = allowed_w.T.astype(ml_dtypes.bfloat16)
    mask01T = np.ascontiguousarray(np.asarray(m01).reshape(4, 128, W))

    wqkvT = np.ascontiguousarray(np.asarray(wd["L0_attn_w_qkv"]).T)
    qk = np.ascontiguousarray(
        wqkvT[:, :4096].reshape(16, 128, 32, 128)
        .transpose(2, 1, 0, 3).reshape(32, 128, 2048))
    wv = np.ascontiguousarray(wqkvT[:, 4096:].reshape(16, 128, 2048))
    woT = np.ascontiguousarray(np.asarray(wd["L0_attn_w_out"]).T)
    wo = np.ascontiguousarray(
        woT.reshape(16, 128, 16, 128).transpose(2, 1, 0, 3).reshape(16, 128, 2048))
    pm = np.asarray(wd["L0_attn_pm"])
    pmk = np.ascontiguousarray(pm[0].transpose(0, 2, 1))
    pmv = np.ascontiguousarray(pm[1])
    wg1T = np.ascontiguousarray(np.asarray(wd["L0_ffn_Wg1"]).T.reshape(16, 128, 8))
    wg2T = np.ascontiguousarray(np.asarray(wd["L0_ffn_Wg2"]).T)
    wup = np.ascontiguousarray(
        np.asarray(wd["L0_ffn_Wup"]).transpose(0, 2, 1)
        .reshape(8, 16, 128, 8, 128).transpose(0, 3, 2, 1, 4).reshape(8, 8, 128, 2048))
    wdT = np.ascontiguousarray(
        np.asarray(wd["L0_ffn_Wdown"]).transpose(0, 2, 1).reshape(8, 8, 128, 2048))
    bup = np.ascontiguousarray(np.asarray(wd["L0_ffn_bup"]).reshape(8, 8, 128, 1))
    bdown = np.ascontiguousarray(np.asarray(wd["L0_ffn_bdown"]))
    normw = np.ascontiguousarray(np.asarray(wd["L0_attn_norm_w"])[None, :])
    m = {
        "xT": xt, "posB": posB, "k2col": k2col, "altcol": altcol,
        "identb": np.asarray(ident), "identf": identf,
        "ones2": np.asarray(ones2),
        "sel8": np.asarray(sel8), "maskqt": maskqt, "mask01T": mask01T,
        "wqk": qk, "wv": wv, "wo": wo, "pmk": pmk, "pmv": pmv,
        "wg1T": wg1T, "wg2T": wg2T, "wupT": wup, "wdT": wdT,
        "bup": bup, "bdown": bdown, "normw": normw,
    }
    if shared is not None:
        for k, v in m.items():
            if k not in ("xT", "posB"):
                shared[k] = v
    return m


def declare_inputs(nc):
    d = {}

    def di(name, shape, dt=F32):
        d[name] = nc.dram_tensor(name, list(shape), dt, kind="ExternalInput").ap()
    di("xT", (D, NT))
    di("posB", (128, NT))
    di("k2col", (128, 1))
    di("altcol", (128, 1))
    di("identb", (128, 128), BF16)
    di("identf", (8, 8))
    di("ones2", (2, 128), BF16)
    di("sel8", (8, 1024), BF16)
    di("maskqt", (4, 128, W))
    di("mask01T", (4, 128, W), BF16)
    di("wqk", (32, 128, 2048))
    di("wv", (16, 128, 2048))
    di("wo", (16, 128, 2048))
    di("pmk", (16, 128, 16))
    di("pmv", (16, 16, 128))
    di("wg1T", (16, 128, 8))
    di("wg2T", (8, 8))
    di("wupT", (8, 8, 128, 2048))
    di("wdT", (8, 8, 128, 2048))
    di("bup", (8, 8, 128, 1))
    di("bdown", (8, 2048))
    di("normw", (1, 2048))
    return d


class MK:
    def __init__(self, stage=6, debug=True):
        self.stage = stage
        self.debug = debug
        self.nc = bass.Bass("TRN2", target_bir_lowering=False, debug=False)
        self.I = declare_inputs(self.nc)
        self.b = Builder(self.nc)
        self.outs = {}
        self.stk = contextlib.ExitStack()

    def out(self, name, shape, dt=F32):
        ap = self.nc.dram_tensor(name, list(shape), dt, kind="ExternalOutput").ap()
        self.outs[name] = ap
        return ap

    def sb(self, name, shape, dt=F32, stk=None):
        return Tile((stk or self.stk).enter_context(
            self.nc.sbuf_tensor(name, list(shape), dt)), name)

    def psb(self, name, shape=(128, 512)):
        return Tile(self.stk.enter_context(
            self.nc.psum_tensor(name, list(shape), F32)), name)

    _dumpn = 0

    def _dump(self, name, tile):
        o = self.out(name, list(tile.t.shape), tile.t.dtype)
        MK._dumpn += 1
        self.b.op("sync", (lambda o=o, t=tile: lambda e: e.dma_start(
            out=o, in_=t.ap))(), reads=[tile], dma=f"d_dbg{MK._dumpn}")

    def _dump_into(self, dst_ap, tile, col=None):
        src = tile.ap if col is None else tile[:, col]
        self.b.dma_out(dst_ap, src, reads=[tile])

    def build(self):
        b, I = self.b, self.I
        self.banks = [self.psb(f"pb{i}") for i in range(8)]

        self.af = self.sb("af_arena", [128, 7696], F32)
        self.ab = self.sb("ab_arena", [128, 53376], BF16)

        def afv(name, a, cols, p=128):
            return Tile(self.af.t[0:p, a:a + cols] if p != 128
                        else self.af.t[:, a:a + cols], name)

        def abv(name, a, cols, p=128):
            return Tile(self.ab.t[0:p, a:a + cols] if p != 128
                        else self.ab.t[:, a:a + cols], name)

        self.x = [self.sb(f"x{i}", [128, NT]) for i in range(NC_F)]
        self.identb = self.sb("identb_s", [128, 128], BF16)
        self.ones2 = self.sb("ones2_s", [2, 128], BF16)
        self.onescol = self.sb("onescol", [128, 2], BF16)
        self.rowb = self.sb("rowbscr", [1, 1024], BF16)
        self.si = self.sb("tscri", [128, 256], I32)
        t_k2 = self.sb("k2_s", [128, 1])
        t_alt = self.sb("alt_s", [128, 1])
        invf = self.sb("invf", [128, 1])
        negmax8 = self.sb("negmax", [128, 8])
        negD8 = self.sb("negD", [128, 8])
        rsum8 = self.sb("rsum", [128, 8])
        locol8 = self.sb("locol", [128, 8])
        hi328 = self.sb("hi32", [128, 8])
        hilo16 = self.sb("hilo", [128, 16], BF16)
        def _sp(t, name):
            return [Tile(t.t[:, 0:t.t.shape[1] // 2], name + "0"),
                    Tile(t.t[:, t.t.shape[1] // 2:], name + "1")]
        self.negmax = _sp(negmax8, "negmax")
        self.negD = _sp(negD8, "negD")
        self.rsum = _sp(rsum8, "rsum")
        self.locol = _sp(locol8, "locol")
        self.hi32 = _sp(hi328, "hi32")
        self.hilo = _sp(hilo16, "hilo")
        self.pmk = [self.sb(f"pmk{h}", [128, NP], BF16) for h in range(NH)]
        self.pmv = [self.sb(f"pmv{h}", [NP, 128], BF16) for h in range(NH)]
        self.gl = self.sb("gl", [128, 8])
        self.gscr = self.sb("gscr", [128, 8])
        self.gm = self.sb("gm", [128, 4])
        self.identf = self.sb("identf_s", [8, 8])
        self.wg2 = self.sb("wg2", [8, 8], BF16)
        self.wg2.dsem = "d_wg2"
        self.bupcol = [self.sb(f"bupc{i}", [128, 1]) for i in range(2)]
        for i, t in enumerate(self.bupcol):
            t.dsem = f"d_bup{i}"
        pmscr = self.sb("pmscr", [128, NP])
        pmscr.dsem = "d_pmscr"

        self.cos_s = afv("cos_s", 0, 1024)
        self.sin_a = afv("sin_a", 1024, 1024)
        t_posB = afv("posB_s", 2048, 1024)
        th = afv("theta", 3072, 1024)
        ts1 = afv("tscr1", 4096, 1024)
        ts2 = afv("tscr2", 5120, 1024)
        self.maskqt = [afv(f"maskqt{i}", 2048 + 512 * i, 512) for i in range(4)]
        self.sim = afv("sim", 4096, 528)
        self.sim1 = afv("sim1", 4624, 528)
        self.rscr = [afv("rscr0", 4624, 512), afv("rscr1", 5136, 512)]
        self.sim1ov = self.sim1
        self.lnscr = afv("lnscr", 5648, 512)
        self.row = afv("rowscr", 6160, 1536, p=1)

        pool64 = [abv(f"bb{i}", 512 * i, 512) for i in range(64)]
        self.qh = pool64[0:16]
        self.kh = pool64[16:32]
        self.xn = pool64[32:48]
        self.attnout = self.xn
        self.vtm = [abv(f"v{tc}", 24576 + 2048 * tc, 2048) for tc in range(4)]
        self.wbuf = [abv(f"wbuf{i}", 32768 + 2048 * i, 2048) for i in range(4)]
        for i, t in enumerate(self.wbuf):
            t.dsem = f"d_wb{i}"
        AE = 40960
        self.mask01 = [abv(f"mask01_{i}", AE + 512 * i, 512) for i in range(4)]
        self.normwb = abv("normwb", AE + 2048, 2048, p=1)
        self.att = [abv(f"att{i}", AE + 4096 + 512 * i, 512) for i in range(4)]
        self.eq = abv("eq", AE + 7168, 528)
        self.attp = abv("attp", AE + 6144, 512, p=NP)
        self.drow = abv("drow", AE + 6656, 512, p=2)
        self.xth = [abv(f"xt{i}", AE + 512 * i, 512) for i in range(NC_F)]
        GB = AE + 8192
        self.sbb = [abv("sbb0", GB, 512), abv("sbb1", GB + 512, 512)]
        self.selb = abv("selb", GB + 1024, 1024, p=8)
        self.selb.dsem = "d_selb"
        self.score_em = abv("score_em", GB + 2048, 1024, p=8)
        self.wg1 = abv("wg1", GB + 3072, 128)
        self.wg1.dsem = "d_wg1"
        self.bdn = abv("bdn", GB + 3200, 1024, p=8)
        self.bdn.dsem = "d_bdn"
        assert GB + 4224 <= 53376, GB + 4224

        for i in range(NC_F):
            b.dma("sync", self.x[i].ap, I["xT"][i * 128:(i + 1) * 128, :],
                  writes=[self.x[i]])
        b.dma("sync", self.identb.ap, I["identb"][:], writes=[self.identb])
        b.dma("sync", self.identf.ap, I["identf"][:], writes=[self.identf])
        b.dma("sync", self.ones2.ap, I["ones2"][:], writes=[self.ones2])
        b.op("dve", lambda v: v.memset(self.onescol.ap, 1.0),
             writes=[self.onescol])
        b.dma("sync", t_posB.ap, I["posB"][:], writes=[t_posB])
        b.dma("sync", t_k2.ap, I["k2col"][:], writes=[t_k2])
        b.dma("sync", t_alt.ap, I["altcol"][:], writes=[t_alt])
        self._trig_tables(t_posB, t_k2, t_alt, invf, th, ts1, ts2)
        b.barrier()
        b.set_init_phase(2)
        b.dma("pool", self.normwb.ap, I["normw"][:], writes=[self.normwb])
        for i in range(4):
            b.dma("sync", self.maskqt[i].ap, I["maskqt"][i],
                  writes=[self.maskqt[i]])
            b.dma("sync", self.mask01[i].ap, I["mask01T"][i],
                  writes=[self.mask01[i]])
        s4c = float(DH) ** -0.25
        for h in range(NH):
            b.dma("sync", pmscr.ap, I["pmk"][h], writes=[pmscr])
            b.op("dve", (lambda h=h: lambda v: v.tensor_single_scalar(
                out=self.pmk[h].ap, in_=pmscr.ap, scalar=s4c, op=ALU.mult))(),
                reads=[pmscr], writes=[self.pmk[h]])
            b.dma("pool", self.pmv[h].ap, I["pmv"][h], writes=[self.pmv[h]])

        if self.debug and self.stage == 0:
            self._dump("dbg_cos", self.cos_s)
            self._dump("dbg_sin", self.sin_a)

        for w in (0, 1):
            self._layernorm(w, fuse_rms=True, xsrc=self.x, dst=self.xn)
            if self.debug and self.stage == 1 and w == 0:
                o = self.out("dbg_xn", [NC_F, 128, W], BF16)
                for i in range(NC_F):
                    self._dump_into(o[i], self.xn[i])
                break
            self._qkv_rope(w)
            if self.debug and self.stage == 2 and w == 0:
                oq = self.out("dbg_q", [NH, 128, W], BF16)
                ok = self.out("dbg_k", [NH, 128, W], BF16)
                ov = self.out("dbg_v", [4, 128, D], BF16)
                for i in range(NH):
                    self._dump_into(oq[i], self.qh[i])
                    self._dump_into(ok[i], self.kh[i])
                for i in range(4):
                    self._dump_into(ov[i], self.vtm[i])
                break
            for h in range(NH + 1):
                if h < NH:
                    self._attn_qt(w, h)
                if h >= 1:
                    self._attn_kt(w, h - 1)
            if self.debug and self.stage == 3 and w == 0:
                o = self.out("dbg_ao", [NC_F, 128, W], BF16)
                for i in range(NC_F):
                    self._dump_into(o[i], self.attnout[i])
                break
            self._outproj(w)

        if self.stage <= 3:
            self.b.finalize()
            return self
        if self.debug and self.stage == 4:
            o = self.out("dbg_x1", [NC_F, 128, NT])
            for i in range(NC_F):
                self._dump_into(o[i], self.x[i])
            self.b.finalize()
            return self

        for t in (self.qh + self.kh + self.xn + self.vtm + self.mask01 +
                  self.att + [self.eq, self.attp, self.drow, self.normwb,
                              self.sim] + self.maskqt):
            t.dead = True
        b.barrier()
        b.set_init_phase(3)
        self._ffn_phase(pool64)
        if self.stage >= 6:
            oy = self.out("y", [D, NT])
            for i in range(NC_F):
                self._dump_into(oy[i * 128:(i + 1) * 128, :], self.x[i])
        self.b.finalize()
        return self

    def _trig_tables(self, t_posB, t_k2, t_alt, invf, th, s1, s2):
        b = self.b
        si = self.si
        b.op("act", lambda s: s.activation(
            out=invf.ap, in_=t_k2.ap, func=AF.Exp,
            scale=-math.log(10000.0) / 128.0), reads=[t_k2], writes=[invf])
        b.op("dve", lambda v: v.tensor_scalar(
            out=th.ap, in0=t_posB.ap, scalar1=invf[:, 0:1], scalar2=None,
            op0=ALU.mult), reads=[t_posB, invf], writes=[th])

        def trig(dst, shift, mult_ap_tile, mult_imm):
            inv2pi = 1.0 / (2.0 * math.pi)
            b.op("dve", lambda v: v.tensor_scalar(
                out=s1.ap, in0=th.ap, scalar1=inv2pi, scalar2=shift * inv2pi,
                op0=ALU.mult, op1=ALU.add), reads=[th], writes=[s1])
            for hf in range(4):
                c = slice(hf * 256, (hf + 1) * 256)
                b.op("dve", (lambda c=c: lambda v: v.tensor_copy(
                    si.ap, s1[:, c]))(), reads=[s1], writes=[si])
                b.op("dve", (lambda c=c: lambda v: v.tensor_copy(
                    s1[:, c], si.ap))(), reads=[si], writes=[s1])
            b.op("dve", lambda v: v.scalar_tensor_tensor(
                out=s2.ap, in0=s1.ap, scalar=-2.0 * math.pi, in1=th.ap,
                op0=ALU.mult, op1=ALU.add), reads=[s1, th], writes=[s2])
            if shift != 0.0:
                b.op("dve", lambda v: v.tensor_single_scalar(
                    out=s2.ap, in_=s2.ap, scalar=shift, op=ALU.add),
                    reads=[s2], writes=[s2])
            b.op("dve", lambda v: v.tensor_single_scalar(
                out=s1.ap, in_=s2.ap, scalar=math.pi, op=ALU.is_gt),
                reads=[s2], writes=[s1])
            b.op("dve", lambda v: v.scalar_tensor_tensor(
                out=s2.ap, in0=s1.ap, scalar=-2.0 * math.pi, in1=s2.ap,
                op0=ALU.mult, op1=ALU.add), reads=[s1, s2], writes=[s2])
            b.op("dve", lambda v: v.tensor_single_scalar(
                out=s1.ap, in_=s2.ap, scalar=-math.pi, op=ALU.is_lt),
                reads=[s2], writes=[s1])
            b.op("dve", lambda v: v.scalar_tensor_tensor(
                out=s2.ap, in0=s1.ap, scalar=2.0 * math.pi, in1=s2.ap,
                op0=ALU.mult, op1=ALU.add), reads=[s1, s2], writes=[s2])
            b.op("dve", lambda v: v.tensor_scalar(
                out=s2.ap, in0=s2.ap, scalar1=math.pi, scalar2=-math.pi,
                op0=ALU.min, op1=ALU.max), reads=[s2], writes=[s2])
            b.op("act", lambda s: s.activation(
                out=s1.ap, in_=s2.ap, func=AF.Sin), reads=[s2], writes=[s1])
            if mult_ap_tile is not None:
                b.op("dve", lambda v: v.tensor_scalar(
                    out=dst.ap, in0=s1.ap, scalar1=mult_ap_tile[:, 0:1],
                    scalar2=None, op0=ALU.mult),
                    reads=[s1, mult_ap_tile], writes=[dst])
            else:
                b.op("dve", lambda v: v.tensor_single_scalar(
                    out=dst.ap, in_=s1.ap, scalar=mult_imm, op=ALU.mult),
                    reads=[s1], writes=[dst])

        s4 = float(DH) ** -0.25
        trig(self.sin_a, 0.0, t_alt, None)
        trig(self.cos_s, math.pi / 2.0, None, s4)

    def _layernorm(self, w, fuse_rms, xsrc, dst, tgt_dt=BF16):
        b = self.b
        sl = slice(w * W, (w + 1) * W)
        S0, S1, PA, PB, PA2, PB2 = self.banks[0:6]
        oc = self.onescol[:, 0:1]
        for i in range(NC_F):
            cb, sq = self.sbb[0], self.sbb[1]
            b.op("dve", (lambda i=i, cb=cb: lambda v: v.tensor_copy(
                cb.ap, xsrc[i][:, sl]))(), reads=[xsrc[i]], writes=[cb])
            b.op("act", (lambda i=i, sq=sq: lambda s: s.activation(
                out=sq.ap, in_=xsrc[i][:, sl], func=AF.Square))(),
                reads=[xsrc[i]], writes=[sq])
            b.mm(S0, S0.t[0:1, :], oc, cb.ap,
                 reads=[self.onescol, cb], start=(i == 0),
                 stop=(i == NC_F - 1), token=True)
            b.mm(S1, S1.t[0:1, :], oc, sq.ap,
                 reads=[self.onescol, sq], start=(i == 0),
                 stop=(i == NC_F - 1), token=True)
        r, rb = self.row, self.rowb
        mrow, erow = r[0:1, 0:512], r[0:1, 512:1024]
        msq = r[0:1, 1024:1536]
        varr, denr = r[0:1, 512:1024], r[0:1, 512:1024]
        alph, malph = r[0:1, 1024:1536], r[0:1, 512:1024]
        b.op("dve", lambda v: v.tensor_scalar(
            out=mrow, in0=S0.t[0:1, :], scalar1=1.0 / D, scalar2=None,
            op0=ALU.mult), reads=[S0], writes=[r])
        b.op("dve", lambda v: v.tensor_scalar(
            out=erow, in0=S1.t[0:1, :], scalar1=1.0 / D, scalar2=None,
            op0=ALU.mult), reads=[S1, r], writes=[r])
        b.op("dve", lambda v: v.tensor_tensor(
            out=msq, in0=mrow, in1=mrow, op=ALU.mult), reads=[r], writes=[r])
        b.op("dve", lambda v: v.tensor_tensor(
            out=varr, in0=erow, in1=msq, op=ALU.subtract),
            reads=[r], writes=[r])
        if fuse_rms:
            b.op("dve", lambda v: v.tensor_scalar(
                out=denr, in0=varr, scalar1=1.0 + 1e-6, scalar2=1e-11,
                op0=ALU.mult, op1=ALU.add), reads=[r], writes=[r])
        else:
            b.op("dve", lambda v: v.tensor_single_scalar(
                out=denr, in_=varr, scalar=1e-5, op=ALU.add),
                reads=[r], writes=[r])
        b.op("act", lambda s: s.activation(out=denr, in_=denr, func=AF.Ln),
             reads=[r], writes=[r])
        b.op("act", lambda s: s.activation(out=alph, in_=denr, func=AF.Exp,
             scale=-0.5), reads=[r], writes=[r])
        b.op("dve", lambda v: v.tensor_tensor(
            out=malph, in0=mrow, in1=alph, op=ALU.mult), reads=[r], writes=[r])
        arow, marow = rb[0:1, 0:512], rb[0:1, 512:1024]
        b.op("dve", lambda v: v.tensor_copy(arow, alph), reads=[r], writes=[rb])
        b.op("dve", lambda v: v.tensor_copy(marow, malph), reads=[r],
             writes=[rb])
        for i in range(NC_F):
            BA = PA if i % 2 == 0 else PA2
            BB = PB if i % 2 == 0 else PB2
            if fuse_rms:
                lh, lht = self.normwb[0:1, i * 128:(i + 1) * 128], self.normwb
            else:
                lh, lht = self.ones2[0:1, :], self.ones2
            b.mm(BA, BA.ap, lh, arow, reads=[lht, self.rowb],
                 start=True, stop=True)
            b.mm(BB, BB.ap, lh, marow, reads=[lht, self.rowb],
                 start=True, stop=True)
            b.op("dve", (lambda i=i, BA=BA: lambda v: v.tensor_tensor(
                out=self.lnscr.ap, in0=xsrc[i][:, sl], in1=BA.ap,
                op=ALU.mult))(), reads=[xsrc[i], BA], writes=[self.lnscr])
            b.op("dve", (lambda i=i, BB=BB: lambda v: v.tensor_tensor(
                out=dst[i].ap, in0=self.lnscr.ap, in1=BB.ap,
                op=ALU.subtract))(), reads=[self.lnscr, BB], writes=[dst[i]])

    def _qkv_rope(self, w):
        b, I = self.b, self.I
        sl = slice(w * W, (w + 1) * W)
        BQ = [self.banks[6], self.banks[7]]
        WB = 32768
        for jp in range(16):
            pa = (jp % 2) * 2
            pair_ap = self.ab.t[:, WB + pa * 2048:WB + (pa + 2) * 2048]
            b.dma("pool", pair_ap.rearrange("p (n j) -> p n j", n=2),
                  I["wqk"][2 * jp:2 * jp + 2].rearrange("n p j -> p n j"),
                  writes=[self.wbuf[pa], self.wbuf[pa + 1]])
            for jl in range(2):
                jc = 2 * jp + jl
                buf = self.wbuf[pa + jl]
                bank = BQ[jc % 2]
                for kc in range(NC_F):
                    b.mm(bank, bank.ap, buf[:, kc * 128:(kc + 1) * 128],
                         self.xn[kc].ap,
                         reads=[buf, self.xn[kc]] if kc == 0
                         else [self.xn[kc]],
                         start=(kc == 0), stop=(kc == NC_F - 1))
                dst = self.qh[jc] if jc < 16 else self.kh[jc - 16]
                self._rope(bank, dst, sl)
        BV = self.banks[0:4]
        WB = 32768
        for jb in range(4):
            for dh8 in range(2):
                pa = ((jb * 2 + dh8) % 2) * 2
                pair_ap = self.ab.t[:, WB + pa * 2048:WB + (pa + 2) * 2048]
                wts = [self.wbuf[pa], self.wbuf[pa + 1]]
                b.dma("pool", pair_ap.rearrange("p (g j) -> p g j", g=8),
                      I["wv"][dh8 * 8:(dh8 + 1) * 8, :,
                              jb * 512:(jb + 1) * 512]
                      .rearrange("g p j -> p g j"), writes=wts)
                for dcl in range(8):
                    dc = dh8 * 8 + dcl
                    vb = wts[dcl // 4]
                    for tc in range(4):
                        b.mm(BV[tc], BV[tc].ap,
                             self.xn[dc][:, tc * 128:(tc + 1) * 128],
                             vb[:, (dcl % 4) * 512:(dcl % 4 + 1) * 512],
                             reads=[self.xn[dc], vb],
                             start=(dc == 0), stop=(dc == 15),
                             token=(dcl % 4 == 3 and tc == 3))
            for tc in range(4):
                b.op("dve", (lambda tc=tc, jb=jb: lambda v: v.tensor_copy(
                    self.vtm[tc][:, jb * 512:(jb + 1) * 512], BV[tc].ap))(),
                    reads=[BV[tc]], writes=[self.vtm[tc]])

    def _rope(self, bank, dst, sl):
        b = self.b
        r0, r1 = self.rscr[0], self.lnscr
        swapmask = [(i // 2) * 2 + (1 - (i % 2)) for i in range(32)]
        b.op("dve", lambda v: v.stream_shuffle(r0.ap, bank.ap, mask=swapmask),
             reads=[bank], writes=[r0, self.sim1ov])
        b.op("dve", (lambda sl=sl: lambda v: v.tensor_tensor(
            out=r1.ap, in0=bank.ap, in1=self.cos_s[:, sl], op=ALU.mult))(),
            reads=[bank, self.cos_s], writes=[r1])
        b.op("dve", (lambda sl=sl: lambda v: v.tensor_tensor(
            out=r0.ap, in0=r0.ap, in1=self.sin_a[:, sl], op=ALU.mult))(),
            reads=[r0, self.sin_a], writes=[r0, self.sim1ov])
        b.op("dve", (lambda dst=dst: lambda v: v.tensor_tensor(
            out=dst.ap, in0=r1.ap, in1=r0.ap, op=ALU.add))(),
            reads=[r0, r1], writes=[dst])

    def _attn_qt(self, w, h):
        b = self.b
        par = h % 2
        q, k = self.qh[h], self.kh[h]
        SA = self.banks[par]
        SB = self.banks[2 + par]
        sim = self.sim if par == 0 else self.sim1
        NM, RS, ND = self.negmax[par], self.rsum[par], self.negD[par]
        LC, H32, HL = self.locol[par], self.hi32[par], self.hilo[par]
        for qc in range(4):
            nmx = NM[:, qc:qc + 1]
            rs = RS[:, qc:qc + 1]
            qsl = q[:, qc * 128:(qc + 1) * 128]
            b.mm(SA, SA.ap, qsl, k.ap, reads=[q, k], start=True, stop=True)
            b.mm(SB, SB.t[:, 0:NP], qsl, self.pmk[h].ap,
                 reads=[q, self.pmk[h]], start=True, stop=True)
            b.op("dve", (lambda sim=sim, SB=SB: lambda v: v.tensor_copy(
                sim[:, 0:NP], SB.t[:, 0:NP]))(),
                reads=[SB], writes=[sim])
            b.op("dve", (lambda qc=qc, sim=sim, SA=SA: lambda v: v.tensor_tensor(
                out=sim[:, NP:], in0=SA.ap, in1=self.maskqt[qc].ap,
                op=ALU.add))(), reads=[SA, self.maskqt[qc]], writes=[sim])
            b.op("dve", (lambda sim=sim, nmx=nmx: lambda v: v.reduce_max(
                out=nmx, in_=sim.ap, axis=AX.X, negate=True))(),
                reads=[sim], writes=[NM])
            b.op("act", (lambda sim=sim, nmx=nmx, rs=rs: lambda s: s.activation(
                out=self.eq.ap, in_=sim.ap, func=AF.Exp, bias=nmx,
                accum_out=rs))(), reads=[sim, NM],
                writes=[self.eq, RS])
        b.op("act", (lambda RS=RS: lambda s: s.activation(
            out=RS.ap, in_=RS.ap, func=AF.Ln))(),
            reads=[RS], writes=[RS])
        b.op("dve", (lambda NM=NM, RS=RS, ND=ND: lambda v: v.tensor_tensor(
            out=ND.ap, in0=NM.ap, in1=RS.ap, op=ALU.subtract))(),
            reads=[NM, RS], writes=[ND])
        hie = HL.t.rearrange("p (a b) -> p a b", b=2)[:, :, 0]
        hio = HL.t.rearrange("p (a b) -> p a b", b=2)[:, :, 1]
        b.op("dve", (lambda hie=hie, ND=ND, HL=HL: lambda v: v.tensor_copy(
            hie, ND.ap))(), reads=[ND], writes=[HL])
        b.op("dve", (lambda hie=hie, H32=H32, HL=HL: lambda v: v.tensor_copy(
            H32.ap, hie))(), reads=[HL], writes=[H32])
        b.op("dve", (lambda ND=ND, H32=H32, LC=LC: lambda v: v.tensor_tensor(
            out=LC.ap, in0=ND.ap, in1=H32.ap, op=ALU.subtract))(),
            reads=[ND, H32], writes=[LC])
        b.op("dve", (lambda hio=hio, LC=LC, HL=HL: lambda v: v.tensor_copy(
            hio, LC.ap))(), reads=[LC], writes=[HL])

    def _attn_kt(self, w, h):
        b = self.b
        par = h % 2
        q, k = self.qh[h], self.kh[h]
        TD = self.banks[4]
        SK = [self.banks[5], self.banks[6]]
        AV = self.banks[7]
        HL = self.hilo[par]
        for qc in range(4):
            b.mm(TD, TD.t[0:2, qc * 128:(qc + 1) * 128],
                 HL[:, 2 * qc:2 * qc + 2],
                 self.identb.ap,
                 reads=[HL, self.identb],
                 start=(qc == 0), stop=(qc == 3))
        b.op("dve", lambda v: v.tensor_copy(self.drow.ap, TD.t[0:2, :]),
             reads=[TD], writes=[self.drow])
        for c in range(5):
            bank = SK[c % 2]
            if c == 0:
                lh_k, lht, cols, dst = self.pmk[h].ap, self.pmk[h], NP, self.attp
            else:
                lh_k = k[:, (c - 1) * 128:c * 128]
                lht, cols, dst = k, 128, self.att[c - 1]
            dstp = bank.t[0:cols, :]
            b.mm(bank, dstp, self.ones2[0:2, 0:cols], self.drow.ap,
                 reads=[self.ones2, self.drow], start=True, stop=False)
            b.mm(bank, dstp, lh_k, q.ap, reads=[lht, q],
                 start=False, stop=True)
            b.op("act", (lambda dstp=dstp, dst=dst: lambda s: s.activation(
                out=dst.ap, in_=dstp, func=AF.Exp))(),
                reads=[bank], writes=[dst])
            if c >= 1:
                b.op("dve", (lambda c=c, dst=dst: lambda v: v.tensor_tensor(
                    out=dst.ap, in0=dst.ap, in1=self.mask01[c - 1].ap,
                    op=ALU.mult))(), reads=[dst, self.mask01[c - 1]],
                    writes=[dst])
        b.mm(AV, AV.ap, self.pmv[h].ap, self.attp.ap,
             reads=[self.pmv[h], self.attp], start=True, stop=False)
        for c in range(1, 5):
            b.mm(AV, AV.ap, self.vtm[c - 1][:, h * 128:(h + 1) * 128],
                 self.att[c - 1].ap, reads=[self.vtm[c - 1], self.att[c - 1]],
                 start=False, stop=(c == 4))
        b.op("dve", (lambda h=h: lambda v: v.tensor_copy(
            self.attnout[h].ap, AV.ap))(), reads=[AV],
            writes=[self.attnout[h]])
        if self.debug and self.stage == 31 and w == 0 and h == 0:
            self._dump("dbg_negmax", self.negmax)
            self._dump("dbg_rsum", self.rsum)
            self._dump("dbg_negD", self.negD)
            self._dump("dbg_hilo", self.hilo)
            self._dump("dbg_drow", self.drow)

    def _outproj(self, w):
        b, I = self.b, self.I
        sl = slice(w * W, (w + 1) * W)
        BO = [self.banks[0], self.banks[1]]
        WB = 32768
        for oc in range(NC_F):
            buf = self.wbuf[(oc % 4) if False else (oc // 2 % 2) * 2 + oc % 2]
            if oc % 2 == 0:
                pa = (oc // 2 % 2) * 2
                pair_ap = self.ab.t[:, WB + pa * 2048:WB + (pa + 2) * 2048]
                b.dma("pool", pair_ap.rearrange("p (n j) -> p n j", n=2),
                      I["wo"][oc:oc + 2].rearrange("n p j -> p n j"),
                      writes=[self.wbuf[pa], self.wbuf[pa + 1]])
            bank = BO[oc % 2]
            for kc in range(NC_F):
                b.mm(bank, bank.ap, buf[:, kc * 128:(kc + 1) * 128],
                     self.attnout[kc].ap,
                     reads=[buf, self.attnout[kc]] if kc == 0
                     else [self.attnout[kc]],
                     start=(kc == 0), stop=(kc == NC_F - 1))
            b.op("dve", (lambda oc=oc, bank=bank: lambda v: v.tensor_tensor(
                out=self.x[oc][:, sl], in0=self.x[oc][:, sl], in1=bank.ap,
                op=ALU.add))(), reads=[bank, self.x[oc]], writes=[self.x[oc]])

    def _ffn_phase(self, pool64):
        b, I = self.b, self.I
        hp = [[Tile(self.ab.t[:, 512 * (8 * e + hc):512 * (8 * e + hc) + 512],
                    f"hp{e}_{hc}") for hc in range(8)] for e in range(NE)]
        self.lgf = Tile(self.af.t[0:8, 2048:2560], "lgf")
        self.g64 = Tile(self.af.t[:, 2560:2592], "g64")
        self.gA = Tile(self.af.t[:, 2592:2624], "gA")
        self.gC = Tile(self.af.t[:, 2624:2656], "gC")
        self.gm4 = Tile(self.af.t[:, 2656:2676], "gm4")
        b.dma("pool", self.wg1.ap.rearrange("p (c e) -> p c e", c=16),
              I["wg1T"].rearrange("c p e -> p c e"), writes=[self.wg1])
        b.dma("pool", self.wg2.ap, I["wg2T"][:], writes=[self.wg2])
        b.dma("sync", self.selb.ap, I["sel8"][:], writes=[self.selb])
        for half in range(2):
            self._layernorm_cols(half, xsrc=self.x, dst_tiles=self.xth)
            self._gate(half)
            if self.debug and self.stage == 45 and half == 0:
                o = self.out("dbg_xt", [NC_F, 128, W], BF16)
                for i in range(NC_F):
                    self._dump_into(o[i], self.xth[i])
                return
            if self.debug and self.stage in (5, 51) and half == (1 if self.stage == 5 else 0):
                self._dump("dbg_scores", self.score_em)
                return
            self._ffn_half(half, hp)

    def _layernorm_cols(self, w, xsrc, dst_tiles):
        b = self.b
        sl = slice(w * W, (w + 1) * W)
        S0, S1, PA, PB, PA2, PB2 = self.banks[0:6]
        oc = self.onescol[:, 0:1]
        for i in range(NC_F):
            cb, sq = self.sbb[0], self.sbb[1]
            b.op("dve", (lambda i=i, cb=cb: lambda v: v.tensor_copy(
                cb.ap, xsrc[i][:, sl]))(), reads=[xsrc[i]], writes=[cb])
            b.op("act", (lambda i=i, sq=sq: lambda s: s.activation(
                out=sq.ap, in_=xsrc[i][:, sl], func=AF.Square))(),
                reads=[xsrc[i]], writes=[sq])
            b.mm(S0, S0.t[0:1, :], oc, cb.ap, reads=[self.onescol, cb],
                 start=(i == 0), stop=(i == NC_F - 1), token=True)
            b.mm(S1, S1.t[0:1, :], oc, sq.ap, reads=[self.onescol, sq],
                 start=(i == 0), stop=(i == NC_F - 1), token=True)
        r, rb = self.row, self.rowb
        mrow, erow = r[0:1, 0:512], r[0:1, 512:1024]
        msq = r[0:1, 1024:1536]
        varr, denr = r[0:1, 512:1024], r[0:1, 512:1024]
        alph, malph = r[0:1, 1024:1536], r[0:1, 512:1024]
        b.op("dve", lambda v: v.tensor_scalar(
            out=mrow, in0=S0.t[0:1, :], scalar1=1.0 / D, scalar2=None,
            op0=ALU.mult), reads=[S0], writes=[r])
        b.op("dve", lambda v: v.tensor_scalar(
            out=erow, in0=S1.t[0:1, :], scalar1=1.0 / D, scalar2=None,
            op0=ALU.mult), reads=[S1, r], writes=[r])
        b.op("dve", lambda v: v.tensor_tensor(
            out=msq, in0=mrow, in1=mrow, op=ALU.mult), reads=[r], writes=[r])
        b.op("dve", lambda v: v.tensor_tensor(
            out=varr, in0=erow, in1=msq, op=ALU.subtract),
            reads=[r], writes=[r])
        b.op("dve", lambda v: v.tensor_single_scalar(
            out=denr, in_=varr, scalar=1e-5, op=ALU.add), reads=[r], writes=[r])
        b.op("act", lambda s: s.activation(out=denr, in_=denr, func=AF.Ln),
             reads=[r], writes=[r])
        b.op("act", lambda s: s.activation(out=alph, in_=denr, func=AF.Exp,
             scale=-0.5), reads=[r], writes=[r])
        b.op("dve", lambda v: v.tensor_tensor(
            out=malph, in0=mrow, in1=alph, op=ALU.mult), reads=[r], writes=[r])
        arow, marow = rb[0:1, 0:512], rb[0:1, 512:1024]
        b.op("dve", lambda v: v.tensor_copy(arow, alph), reads=[r], writes=[rb])
        b.op("dve", lambda v: v.tensor_copy(marow, malph), reads=[r],
             writes=[rb])
        for i in range(NC_F):
            BA = PA if i % 2 == 0 else PA2
            BB = PB if i % 2 == 0 else PB2
            lh, lht = self.ones2[0:1, :], self.ones2
            b.mm(BA, BA.ap, lh, arow, reads=[lht, self.rowb],
                 start=True, stop=True)
            b.mm(BB, BB.ap, lh, marow, reads=[lht, self.rowb],
                 start=True, stop=True)
            b.op("dve", (lambda i=i, BA=BA: lambda v: v.tensor_tensor(
                out=self.lnscr.ap, in0=xsrc[i][:, sl], in1=BA.ap,
                op=ALU.mult))(), reads=[xsrc[i], BA], writes=[self.lnscr])
            b.op("dve", (lambda i=i, BB=BB: lambda v: v.tensor_tensor(
                out=dst_tiles[i].ap, in0=self.lnscr.ap, in1=BB.ap,
                op=ALU.subtract))(), reads=[self.lnscr, BB],
                writes=[dst_tiles[i]])

    def _gate(self, half):
        b, I = self.b, self.I
        hsl = slice(half * 512, (half + 1) * 512)
        G1, G2 = self.banks[6], self.banks[7]
        for kc in range(NC_F):
            b.mm(G1, G1.t[0:8, :], self.wg1[:, kc * 8:(kc + 1) * 8],
                 self.xth[kc].ap, reads=[self.wg1, self.xth[kc]],
                 start=(kc == 0), stop=(kc == NC_F - 1))
        b.op("act", lambda s: s.activation(
            out=self.score_em[0:8, hsl], in_=G1.t[0:8, :], func=AF.Tanh),
            reads=[G1], writes=[self.score_em])
        b.mm(G2, G2.t[0:8, :], self.wg2.ap, self.score_em[0:8, hsl],
             reads=[self.wg2, self.score_em], start=True, stop=True)
        b.op("dve", lambda v: v.tensor_copy(
            self.lgf.ap, G2.t[0:8, :]), reads=[G2],
            writes=[self.lgf])
        if self.debug and self.stage == 51 and half == 0:
            self._dump("dbg_logits", self.lgf)
        TS = self.banks[6]
        for tc in range(4):
            b.mm(TS, TS.t[0:128, 8 * tc:8 * tc + 8],
                 self.lgf[0:8, tc * 128:(tc + 1) * 128], self.identf.ap,
                 reads=[self.lgf, self.identf],
                 start=(tc == 0), stop=(tc == 3))
        g64, gA, gC, gm4 = self.g64, self.gA, self.gC, self.gm4
        m1, m2 = gm4[:, 0:4], gm4[:, 4:8]
        s1, s2 = gm4[:, 8:12], gm4[:, 12:16]

        def e3(ap):
            return ap.rearrange("p (t e) -> p t e", e=8)

        def b3(ap):
            return ap.rearrange("p (t o) -> p t o", o=1).broadcast_to((128, 4, 8))

        b.op("dve", lambda v: v.tensor_copy(g64.ap, TS.t[0:128, 0:32]),
             reads=[TS], writes=[g64])
        b.op("dve", lambda v: v.tensor_reduce(
            out=m1, in_=e3(g64.ap), axis=AX.X, op=ALU.max, negate=True),
            reads=[g64], writes=[gm4])
        b.op("dve", lambda v: v.tensor_tensor(
            out=e3(gA.ap), in0=e3(g64.ap), in1=b3(m1), op=ALU.add),
            reads=[g64, gm4], writes=[gA])
        b.op("dve", lambda v: v.tensor_single_scalar(
            out=gA.ap, in_=gA.ap, scalar=0.0, op=ALU.is_ge),
            reads=[gA], writes=[gA])
        b.op("dve", lambda v: v.scalar_tensor_tensor(
            out=g64.ap, in0=gA.ap, scalar=-1e30, in1=g64.ap,
            op0=ALU.mult, op1=ALU.add), reads=[gA, g64], writes=[g64])
        b.op("dve", lambda v: v.tensor_reduce(
            out=m2, in_=e3(g64.ap), axis=AX.X, op=ALU.max, negate=True),
            reads=[g64], writes=[gm4])
        b.op("dve", lambda v: v.tensor_tensor(
            out=e3(gC.ap), in0=e3(g64.ap), in1=b3(m2), op=ALU.add),
            reads=[g64, gm4], writes=[gC])
        b.op("dve", lambda v: v.tensor_single_scalar(
            out=gC.ap, in_=gC.ap, scalar=0.0, op=ALU.is_ge),
            reads=[gC], writes=[gC])
        b.op("dve", lambda v: v.tensor_tensor(
            out=s1, in0=m2, in1=m1, op=ALU.subtract),
            reads=[gm4], writes=[gm4])
        b.op("act", lambda s: s.activation(out=s1, in_=s1, func=AF.Sigmoid),
             reads=[gm4], writes=[gm4])
        b.op("dve", lambda v: v.tensor_scalar(
            out=s2, in0=s1, scalar1=-1.0, scalar2=1.0,
            op0=ALU.mult, op1=ALU.add), reads=[gm4], writes=[gm4])
        b.op("dve", lambda v: v.tensor_tensor(
            out=e3(gA.ap), in0=e3(gA.ap), in1=b3(s1), op=ALU.mult),
            reads=[gA, gm4], writes=[gA])
        b.op("dve", lambda v: v.tensor_tensor(
            out=e3(gC.ap), in0=e3(gC.ap), in1=b3(s2), op=ALU.mult),
            reads=[gC, gm4], writes=[gC])
        b.op("dve", lambda v: v.tensor_tensor(
            out=gA.ap, in0=gA.ap, in1=gC.ap, op=ALU.add),
            reads=[gA, gC], writes=[gA])
        b.op("dve", lambda v: v.tensor_copy(self.sbb[0][:, 0:32], gA.ap),
             reads=[gA], writes=[self.sbb[0]])
        G2b = self.banks[7]
        for tc in range(4):
            tsl = slice(half * 512 + tc * 128, half * 512 + (tc + 1) * 128)
            b.mm(G2b, G2b.t[0:8, 0:128],
                 self.sbb[0][:, 8 * tc:8 * tc + 8],
                 self.identb[:, 0:128], reads=[self.sbb[0], self.identb],
                 start=True, stop=True)
            b.op("dve", (lambda tsl=tsl, G2b=G2b: lambda v: v.tensor_copy(
                self.score_em[0:8, tsl], G2b.t[0:8, 0:128]))(),
                reads=[G2b], writes=[self.score_em])

    def _ffn_half(self, half, hp):
        b, I = self.b, self.I
        hsl = slice(half * 512, (half + 1) * 512)
        scoreB = self.rscr[0]
        silu2 = [self.rscr[1], self.lnscr]
        wb = self.wbuf
        WB = 32768
        UPP = [self.banks[6], self.banks[7]]
        SCB = self.banks[5]
        for e in range(NE):
            b.mm(SCB, SCB.ap, self.selb[0:8, e * 128:(e + 1) * 128],
                 self.score_em[0:8, hsl], reads=[self.selb, self.score_em],
                 start=True, stop=True)
            b.op("dve", lambda v: v.tensor_copy(
                scoreB.ap, SCB.ap), reads=[SCB], writes=[scoreB])
            for hc in range(8):
                if hc % 2 == 0:
                    pa = (hc // 2 % 2) * 2
                    pair_ap = self.ab.t[:, WB + pa * 2048:WB + (pa + 2) * 2048]
                    b.dma("pool", pair_ap.rearrange("p (n j) -> p n j", n=2),
                          I["wupT"][e, hc:hc + 2].rearrange("n p j -> p n j"),
                          writes=[wb[pa], wb[pa + 1]])
                buf = wb[(hc // 2 % 2) * 2 + hc % 2]
                bc = self.bupcol[hc % 2]
                b.dma("sync", bc.ap, I["bup"][e, hc], writes=[bc])
                UP = UPP[hc % 2]
                sscr = silu2[hc % 2]
                for kc in range(NC_F):
                    b.mm(UP, UP.ap, buf[:, kc * 128:(kc + 1) * 128],
                         self.xth[kc].ap,
                         reads=[buf, self.xth[kc]] if kc == 0
                         else [self.xth[kc]],
                         start=(kc == 0), stop=(kc == NC_F - 1))
                b.op("act", (lambda bc=bc, UP=UP, sscr=sscr:
                     lambda s: s.activation(
                         out=sscr.ap, in_=UP.ap, func=AF.Silu,
                         bias=bc[:, 0:1]))(), reads=[UP, bc],
                    writes=[sscr])
                b.op("dve", (lambda e=e, hc=hc, sscr=sscr:
                     lambda v: v.tensor_tensor(
                         out=hp[e][hc].ap, in0=sscr.ap,
                         in1=scoreB.ap, op=ALU.mult))(),
                    reads=[sscr, scoreB], writes=[hp[e][hc]])
        for og in range(2):
            DB = self.banks
            b.dma("pool", self.bdn.ap,
                  I["bdown"][0:8, og * 1024:(og + 1) * 1024],
                  writes=[self.bdn])
            for ocl in range(8):
                b.mm(DB[ocl], DB[ocl].ap,
                     self.bdn[0:8, ocl * 128:(ocl + 1) * 128],
                     self.score_em[0:8, hsl],
                     reads=[self.bdn, self.score_em],
                     start=True, stop=False)
            for e in range(NE):
                for hc in range(8):
                    if hc % 4 == 0:
                        pa = (hc // 4 % 2) * 2
                        pair_ap = self.ab.t[:, WB + pa * 2048:
                                            WB + (pa + 2) * 2048]
                        b.dma("pool",
                              pair_ap.rearrange("p (n j) -> p n j", n=4),
                              I["wdT"][e, hc:hc + 4, :,
                                       og * 1024:(og + 1) * 1024]
                              .rearrange("n p j -> p n j"),
                              writes=[wb[pa], wb[pa + 1]])
                    buf = wb[(hc // 4 % 2) * 2 + (hc % 4) // 2]
                    boff = (hc % 2) * 1024
                    for ocl in range(8):
                        last = (e == NE - 1 and hc == 7)
                        b.mm(DB[ocl], DB[ocl].ap,
                             buf[:, boff + ocl * 128:boff + (ocl + 1) * 128],
                             hp[e][hc].ap,
                             reads=[buf, hp[e][hc]] if ocl in (0, 7)
                             else [hp[e][hc]],
                             start=False, stop=last,
                             token=(ocl == 7))
            for ocl in range(8):
                oc = og * 8 + ocl
                b.op("dve", (lambda oc=oc, ocl=ocl, hsl=hsl, DB=DB:
                     lambda v: v.tensor_tensor(
                         out=self.x[oc][:, hsl], in0=self.x[oc][:, hsl],
                         in1=DB[ocl].ap, op=ALU.add))(),
                    reads=[DB[ocl], self.x[oc]], writes=[self.x[oc]])


from concourse.bass_utils import run_bass_kernel_spmd


def kernel(**inputs):
    mk = MK(stage=6, debug=False).build()
    shared = {}
    in_maps = [host_inputs(inputs["L0_attn_x"], inputs, c, shared=shared)
               for c in range(8)]
    res = run_bass_kernel_spmd(mk.nc, in_maps, core_ids=list(range(8)))
    ys = [np.asarray(r["y"]).T for r in res.results]
    return np.concatenate(ys, axis=0).reshape(2, 4096, 2048).astype(np.float32)


# version 09
# speedup vs baseline: 1.0783x; 1.0783x over previous
import math
import contextlib
import numpy as np

import concourse.bass as bass
import concourse.mybir as mybir

F32 = mybir.dt.float32
BF16 = mybir.dt.bfloat16
I32 = mybir.dt.int32
AF = mybir.ActivationFunctionType
ALU = mybir.AluOpType
AX = mybir.AxisListType

D = 2048
NT = 1024
W = 512
NH = 16
DH = 128
NP = 16
KT = NP + W
NC_F = D // 128
HE = 1024
NE = 8
NEG = -np.float32(3.4028235e38)

ENGS = ("pe", "dve", "act", "pool", "sync")
PROMOTE = ("d_init", "d_initp", "d_init2", "d_initp2", "d_init3", "d_initp3")


class Tile:
    __slots__ = ("t", "name", "last_w", "readers", "pending", "dead", "dsem")

    def __init__(self, t, name):
        self.t = t
        self.name = name
        self.last_w = None
        self.readers = []
        self.pending = []
        self.dead = False
        self.dsem = None

    def __getitem__(self, idx):
        return self.t[idx]

    @property
    def ap(self):
        import bass_rust
        if isinstance(self.t, bass_rust.AP):
            return self.t
        return self.t[:]


class Builder:
    def __init__(self, nc):
        self.nc = nc
        self.ops = {e: [] for e in ENGS}
        import collections
        self.cnt = collections.defaultdict(int)
        for s in ("pe", "dve", "act", "pool"):
            self.cnt[s] = 0
        self.nops = 0
        self.init_sem = {"sync": "d_init", "pool": "d_initp"}
        self.barrier_pending = {}

    def set_init_phase(self, n):
        self.init_sem = {"sync": f"d_init{n}", "pool": f"d_initp{n}"}

    def barrier(self):
        snap = [(s, v) for s, v in self.cnt.items() if v > 0]
        for e in ENGS:
            self.barrier_pending[e] = list(snap)

    def _take_barrier(self, eng):
        w = self.barrier_pending.pop(eng, [])
        return w

    def op(self, eng, fn, reads=(), writes=(), extra=(), dma=False):
        waits = self._take_barrier(eng) + list(extra)
        for t in reads:
            assert not t.dead, t.name
            if t.last_w is not None:
                waits.append(t.last_w)
        for t in writes:
            assert not t.dead, t.name
            if t.last_w is not None:
                waits.append(t.last_w)
            waits.extend(t.readers)
        if dma:
            sem = dma if isinstance(dma, str) else self.init_sem[eng]
            inc = 16
        else:
            sem, inc = eng, 1
        self.cnt[sem] += inc
        tok = (sem, self.cnt[sem])
        self.ops[eng].append((fn, tuple(waits), (sem, inc)))
        self.nops += 1
        for t in reads:
            t.readers.append(tok)
        for t in writes:
            t.last_w = tok
            t.readers = [tok]
        return tok

    def mm(self, bank, out_ap, lh, rh, reads=(), start=False, stop=False,
           token=False, extra=()):
        waits = self._take_barrier("pe") + list(extra)
        if start:
            if bank.last_w is not None:
                waits.append(bank.last_w)
            waits.extend(bank.readers)
            bank.pending = []
        for t in reads:
            assert not t.dead, t.name
            if t.last_w is not None:
                waits.append(t.last_w)
            bank.pending.append(t)
        tok, inc = None, None
        if stop or token:
            self.cnt["pe"] += 1
            tok = ("pe", self.cnt["pe"])
            inc = ("pe", 1)
        self.ops["pe"].append((
            (lambda o=out_ap, a=lh, b=rh, s=start, p=stop: (
                lambda t: t.matmul(o, a, b, start=s, stop=p)))(),
            tuple(waits), inc))
        self.nops += 1
        if tok is not None:
            for t in bank.pending:
                t.readers.append(tok)
            bank.pending = []
        if stop:
            bank.last_w = tok
            bank.readers = [tok]
        return tok

    def mmg(self, bank, mms, reads=()):
        n = len(mms)
        tok = None
        for i, (o, lh, rh) in enumerate(mms):
            tok = self.mm(bank, o, lh, rh, reads=reads if i == 0 else (),
                          start=(i == 0), stop=(i == n - 1))
        return tok

    def dma(self, eng, out_ap, in_ap, reads=(), writes=(), extra=()):
        sem = True
        for t in writes:
            if t.dsem:
                sem = t.dsem
        return self.op(eng, lambda e: e.dma_start(out=out_ap, in_=in_ap),
                       reads=reads, writes=writes, extra=extra, dma=sem)

    def dma_out(self, out_ap, in_ap, reads=()):
        return self.op("sync", lambda e: e.dma_start(out=out_ap, in_=in_ap),
                       reads=reads, dma="d_out")

    def finalize(self):
        nc = self.nc
        finals = dict(self.cnt)
        with contextlib.ExitStack() as st:
            sems = {s: st.enter_context(nc.semaphore(s))
                    for s in self.cnt if self.cnt[s] > 0}
            blk = st.enter_context(nc.Block())

            def emit(eng_obj, stream, last_wait=None):
                done = {}
                for fn, waits, inc in stream:
                    byname = {}
                    for (s, v) in waits:
                        if s in PROMOTE:
                            v = finals[s]
                        if done.get(s, 0) < v and byname.get(s, 0) < v:
                            byname[s] = v
                    for s, v in sorted(byname.items()):
                        eng_obj.wait_ge(sems[s], v)
                        done[s] = v
                    inst = fn(eng_obj)
                    if inc is not None:
                        s, amt = inc
                        inst.then_inc(sems[s], amt)
                if last_wait is not None and finals.get(last_wait, 0) > 0:
                    eng_obj.wait_ge(sems[last_wait], finals[last_wait])

            @blk.tensor
            def _(t):
                emit(t, self.ops["pe"])

            @blk.vector
            def _(v):
                emit(v, self.ops["dve"])

            @blk.scalar
            def _(s):
                emit(s, self.ops["act"])

            @blk.gpsimd
            def _(g):
                emit(g, self.ops["pool"])

            @blk.sync
            def _(s):
                emit(s, self.ops["sync"], last_wait="d_out")


def host_inputs(x_full, wd, core, shared=None):
    import ml_dtypes
    c = core
    if shared is not None and shared:
        m = dict(shared)
        xt = np.ascontiguousarray(
            np.asarray(x_full).reshape(8192, D)[c * NT:(c + 1) * NT, :].T)
        rows = np.arange(c * NT, (c + 1) * NT)
        pos = (rows % 4096).astype(np.float32)
        m["xT"] = xt
        m["posB"] = np.ascontiguousarray(
            np.broadcast_to(pos[None, :], (128, NT)))
        return m
    xt = np.ascontiguousarray(
        np.asarray(x_full).reshape(8192, D)[c * NT:(c + 1) * NT, :].T)
    rows = np.arange(c * NT, (c + 1) * NT)
    pos = (rows % 4096).astype(np.float32)
    posB = np.ascontiguousarray(np.broadcast_to(pos[None, :], (128, NT)))
    p = np.arange(128)
    k2col = (2 * (p // 2)).astype(np.float32)[:, None]
    s4 = float(DH) ** -0.25
    altcol = (np.where(p % 2 == 1, 1.0, -1.0) * s4).astype(np.float32)[:, None]
    ident = np.eye(128, dtype=ml_dtypes.bfloat16)
    identf = np.eye(8, dtype=np.float32)
    ones2 = np.ones((2, 128), dtype=ml_dtypes.bfloat16)
    sel8 = np.zeros((8, 8 * 128), dtype=ml_dtypes.bfloat16)
    for e in range(8):
        sel8[e, e * 128:(e + 1) * 128] = 1.0
    qt = np.arange(W)[:, None]
    ktw = np.arange(W)[None, :]
    allowed_w = (ktw < 16) | ((qt >= 16) & (qt >= ktw))
    maskqt = np.zeros((W, KT), dtype=np.float32)
    maskqt[:, NP:] = np.where(allowed_w, 0.0, NEG)
    maskqt = np.ascontiguousarray(maskqt.reshape(4, 128, KT))
    m01 = allowed_w.T.astype(ml_dtypes.bfloat16)
    mask01T = np.ascontiguousarray(np.asarray(m01).reshape(4, 128, W))

    wqkvT = np.ascontiguousarray(np.asarray(wd["L0_attn_w_qkv"]).T)
    qk = np.ascontiguousarray(
        wqkvT[:, :4096].reshape(16, 128, 32, 128)
        .transpose(2, 1, 0, 3).reshape(32, 128, 2048))
    wv = np.ascontiguousarray(wqkvT[:, 4096:].reshape(16, 128, 2048))
    woT = np.ascontiguousarray(np.asarray(wd["L0_attn_w_out"]).T)
    wo = np.ascontiguousarray(
        woT.reshape(16, 128, 16, 128).transpose(2, 1, 0, 3).reshape(16, 128, 2048))
    pm = np.asarray(wd["L0_attn_pm"])
    pmk = np.ascontiguousarray(pm[0].transpose(0, 2, 1))
    pmv = np.ascontiguousarray(pm[1])
    wg1T = np.ascontiguousarray(np.asarray(wd["L0_ffn_Wg1"]).T.reshape(16, 128, 8))
    wg2T = np.ascontiguousarray(np.asarray(wd["L0_ffn_Wg2"]).T)
    wup = np.ascontiguousarray(
        np.asarray(wd["L0_ffn_Wup"]).transpose(0, 2, 1)
        .reshape(8, 16, 128, 8, 128).transpose(0, 3, 2, 1, 4).reshape(8, 8, 128, 2048))
    wdT = np.ascontiguousarray(
        np.asarray(wd["L0_ffn_Wdown"]).transpose(0, 2, 1).reshape(8, 8, 128, 2048))
    bup = np.ascontiguousarray(np.asarray(wd["L0_ffn_bup"]).reshape(8, 8, 128, 1))
    bdown = np.ascontiguousarray(np.asarray(wd["L0_ffn_bdown"]))
    normw = np.ascontiguousarray(np.asarray(wd["L0_attn_norm_w"])[None, :])
    m = {
        "xT": xt, "posB": posB, "k2col": k2col, "altcol": altcol,
        "identb": np.asarray(ident), "identf": identf,
        "ones2": np.asarray(ones2),
        "sel8": np.asarray(sel8), "maskqt": maskqt, "mask01T": mask01T,
        "wqk": qk, "wv": wv, "wo": wo, "pmk": pmk, "pmv": pmv,
        "wg1T": wg1T, "wg2T": wg2T, "wupT": wup, "wdT": wdT,
        "bup": bup, "bdown": bdown, "normw": normw,
    }
    if shared is not None:
        for k, v in m.items():
            if k not in ("xT", "posB"):
                shared[k] = v
    return m


def declare_inputs(nc):
    d = {}

    def di(name, shape, dt=F32):
        d[name] = nc.dram_tensor(name, list(shape), dt, kind="ExternalInput").ap()
    di("xT", (D, NT))
    di("posB", (128, NT))
    di("k2col", (128, 1))
    di("altcol", (128, 1))
    di("identb", (128, 128), BF16)
    di("identf", (8, 8))
    di("ones2", (2, 128), BF16)
    di("sel8", (8, 1024), BF16)
    di("maskqt", (4, 128, KT))
    di("mask01T", (4, 128, W), BF16)
    di("wqk", (32, 128, 2048))
    di("wv", (16, 128, 2048))
    di("wo", (16, 128, 2048))
    di("pmk", (16, 128, 16))
    di("pmv", (16, 16, 128))
    di("wg1T", (16, 128, 8))
    di("wg2T", (8, 8))
    di("wupT", (8, 8, 128, 2048))
    di("wdT", (8, 8, 128, 2048))
    di("bup", (8, 8, 128, 1))
    di("bdown", (8, 2048))
    di("normw", (1, 2048))
    return d


class MK:
    def __init__(self, stage=6, debug=True):
        self.stage = stage
        self.debug = debug
        self.nc = bass.Bass("TRN2", target_bir_lowering=False, debug=False)
        self.I = declare_inputs(self.nc)
        self.b = Builder(self.nc)
        self.outs = {}
        self.stk = contextlib.ExitStack()

    def out(self, name, shape, dt=F32):
        ap = self.nc.dram_tensor(name, list(shape), dt, kind="ExternalOutput").ap()
        self.outs[name] = ap
        return ap

    def sb(self, name, shape, dt=F32, stk=None):
        return Tile((stk or self.stk).enter_context(
            self.nc.sbuf_tensor(name, list(shape), dt)), name)

    def psb(self, name, shape=(128, 512)):
        return Tile(self.stk.enter_context(
            self.nc.psum_tensor(name, list(shape), F32)), name)

    _dumpn = 0

    def _dump(self, name, tile):
        o = self.out(name, list(tile.t.shape), tile.t.dtype)
        MK._dumpn += 1
        self.b.op("sync", (lambda o=o, t=tile: lambda e: e.dma_start(
            out=o, in_=t.ap))(), reads=[tile], dma=f"d_dbg{MK._dumpn}")

    def _dump_into(self, dst_ap, tile, col=None):
        src = tile.ap if col is None else tile[:, col]
        self.b.dma_out(dst_ap, src, reads=[tile])

    def build(self):
        b, I = self.b, self.I
        self.banks = [self.psb(f"pb{i}") for i in range(8)]

        self.af = self.sb("af_arena", [128, 7760], F32)
        self.ab = self.sb("ab_arena", [128, 53376], BF16)

        def afv(name, a, cols, p=128):
            return Tile(self.af.t[0:p, a:a + cols] if p != 128
                        else self.af.t[:, a:a + cols], name)

        def abv(name, a, cols, p=128):
            return Tile(self.ab.t[0:p, a:a + cols] if p != 128
                        else self.ab.t[:, a:a + cols], name)

        self.x = [self.sb(f"x{i}", [128, NT]) for i in range(NC_F)]
        self.identb = self.sb("identb_s", [128, 128], BF16)
        self.ones2 = self.sb("ones2_s", [2, 128], BF16)
        self.onescol = self.sb("onescol", [128, 2], BF16)
        self.rowb = self.sb("rowbscr", [1, 1024], BF16)
        self.si = self.sb("tscri", [128, 256], I32)
        t_k2 = self.sb("k2_s", [128, 1])
        t_alt = self.sb("alt_s", [128, 1])
        invf = self.sb("invf", [128, 1])
        negmax8 = self.sb("negmax", [128, 8])
        negD8 = self.sb("negD", [128, 8])
        rsum8 = self.sb("rsum", [128, 8])
        locol8 = self.sb("locol", [128, 8])
        hi328 = self.sb("hi32", [128, 8])
        hilo16 = self.sb("hilo", [128, 16], BF16)
        def _sp(t, name):
            return [Tile(t.t[:, 0:t.t.shape[1] // 2], name + "0"),
                    Tile(t.t[:, t.t.shape[1] // 2:], name + "1")]
        self.negmax = _sp(negmax8, "negmax")
        self.negD = _sp(negD8, "negD")
        self.rsum = _sp(rsum8, "rsum")
        self.locol = _sp(locol8, "locol")
        self.hi32 = _sp(hi328, "hi32")
        self.hilo = _sp(hilo16, "hilo")
        self.pmk = [self.sb(f"pmk{h}", [128, NP], BF16) for h in range(NH)]
        self.pmv = [self.sb(f"pmv{h}", [NP, 128], BF16) for h in range(NH)]
        self.gl = self.sb("gl", [128, 8])
        self.gscr = self.sb("gscr", [128, 8])
        self.gm = self.sb("gm", [128, 4])
        self.identf = self.sb("identf_s", [8, 8])
        self.wg2 = self.sb("wg2", [8, 8], BF16)
        self.wg2.dsem = "d_wg2"
        self.bupcol = [self.sb(f"bupc{i}", [128, 1]) for i in range(2)]
        for i, t in enumerate(self.bupcol):
            t.dsem = f"d_bup{i}"
        pmscr = self.sb("pmscr", [128, NP])
        pmscr.dsem = "d_pmscr"

        self.cos_s = afv("cos_s", 0, 1024)
        self.sin_a = afv("sin_a", 1024, 1024)
        t_posB = afv("posB_s", 2048, 1024)
        th = afv("theta", 3072, 1024)
        ts1 = afv("tscr1", 4096, 1024)
        ts2 = afv("tscr2", 5120, 1024)
        self.maskqt = [afv(f"maskqt{i}", 2048 + 528 * i, 528) for i in range(4)]
        self.sim = afv("sim", 4160, 528)
        self.sim1 = afv("sim1", 4688, 528)
        self.rscr = [afv("rscr0", 4688, 512), afv("rscr1", 5200, 512)]
        self.sim1ov = self.sim1
        self.lnscr = afv("lnscr", 5712, 512)
        self.row = afv("rowscr", 6224, 1536, p=1)

        pool64 = [abv(f"bb{i}", 512 * i, 512) for i in range(64)]
        self.qh = pool64[0:16]
        self.kh = pool64[16:32]
        self.xn = pool64[32:48]
        self.attnout = self.xn
        self.vtm = [abv(f"v{tc}", 24576 + 2048 * tc, 2048) for tc in range(4)]
        self.wbuf = [abv(f"wbuf{i}", 32768 + 2048 * i, 2048) for i in range(4)]
        for i, t in enumerate(self.wbuf):
            t.dsem = f"d_wb{i}"
        AE = 40960
        self.mask01 = [abv(f"mask01_{i}", AE + 512 * i, 512) for i in range(4)]
        self.normwb = abv("normwb", AE + 2048, 2048, p=1)
        self.att = [abv(f"att{i}", AE + 4096 + 512 * i, 512) for i in range(4)]
        self.eq = abv("eq", AE + 7168, 528)
        self.attp = abv("attp", AE + 6144, 512, p=NP)
        self.drow = abv("drow", AE + 6656, 512, p=2)
        self.xth = [abv(f"xt{i}", AE + 512 * i, 512) for i in range(NC_F)]
        GB = AE + 8192
        self.sbb = [abv("sbb0", GB, 512), abv("sbb1", GB + 512, 512)]
        self.selb = abv("selb", GB + 1024, 1024, p=8)
        self.selb.dsem = "d_selb"
        self.score_em = abv("score_em", GB + 2048, 1024, p=8)
        self.wg1 = abv("wg1", GB + 3072, 128)
        self.wg1.dsem = "d_wg1"
        self.bdn = abv("bdn", GB + 3200, 1024, p=8)
        self.bdn.dsem = "d_bdn"
        assert GB + 4224 <= 53376, GB + 4224

        for i in range(NC_F):
            b.dma("sync", self.x[i].ap, I["xT"][i * 128:(i + 1) * 128, :],
                  writes=[self.x[i]])
        b.dma("sync", self.identb.ap, I["identb"][:], writes=[self.identb])
        b.dma("sync", self.identf.ap, I["identf"][:], writes=[self.identf])
        b.dma("sync", self.ones2.ap, I["ones2"][:], writes=[self.ones2])
        b.op("dve", lambda v: v.memset(self.onescol.ap, 1.0),
             writes=[self.onescol])
        b.dma("sync", t_posB.ap, I["posB"][:], writes=[t_posB])
        b.dma("sync", t_k2.ap, I["k2col"][:], writes=[t_k2])
        b.dma("sync", t_alt.ap, I["altcol"][:], writes=[t_alt])
        self._trig_tables(t_posB, t_k2, t_alt, invf, th, ts1, ts2)
        b.barrier()
        b.set_init_phase(2)
        b.dma("pool", self.normwb.ap, I["normw"][:], writes=[self.normwb])
        for i in range(4):
            b.dma("sync", self.maskqt[i].ap, I["maskqt"][i],
                  writes=[self.maskqt[i]])
            b.dma("sync", self.mask01[i].ap, I["mask01T"][i],
                  writes=[self.mask01[i]])
        s4c = float(DH) ** -0.25
        for h in range(NH):
            b.dma("sync", pmscr.ap, I["pmk"][h], writes=[pmscr])
            b.op("dve", (lambda h=h: lambda v: v.tensor_single_scalar(
                out=self.pmk[h].ap, in_=pmscr.ap, scalar=s4c, op=ALU.mult))(),
                reads=[pmscr], writes=[self.pmk[h]])
            b.dma("pool", self.pmv[h].ap, I["pmv"][h], writes=[self.pmv[h]])

        if self.debug and self.stage == 0:
            self._dump("dbg_cos", self.cos_s)
            self._dump("dbg_sin", self.sin_a)

        for w in (0, 1):
            self._layernorm(w, fuse_rms=True, xsrc=self.x, dst=self.xn)
            if self.debug and self.stage == 1 and w == 0:
                o = self.out("dbg_xn", [NC_F, 128, W], BF16)
                for i in range(NC_F):
                    self._dump_into(o[i], self.xn[i])
                break
            self._qkv_rope(w)
            if self.debug and self.stage == 2 and w == 0:
                oq = self.out("dbg_q", [NH, 128, W], BF16)
                ok = self.out("dbg_k", [NH, 128, W], BF16)
                ov = self.out("dbg_v", [4, 128, D], BF16)
                for i in range(NH):
                    self._dump_into(oq[i], self.qh[i])
                    self._dump_into(ok[i], self.kh[i])
                for i in range(4):
                    self._dump_into(ov[i], self.vtm[i])
                break
            for h in range(NH + 1):
                if h < NH:
                    self._attn_qt(w, h)
                if h >= 1:
                    self._attn_kt(w, h - 1)
            if self.debug and self.stage == 3 and w == 0:
                o = self.out("dbg_ao", [NC_F, 128, W], BF16)
                for i in range(NC_F):
                    self._dump_into(o[i], self.attnout[i])
                break
            self._outproj(w)

        if self.stage <= 3:
            self.b.finalize()
            return self
        if self.debug and self.stage == 4:
            o = self.out("dbg_x1", [NC_F, 128, NT])
            for i in range(NC_F):
                self._dump_into(o[i], self.x[i])
            self.b.finalize()
            return self

        for t in (self.qh + self.kh + self.xn + self.vtm + self.mask01 +
                  self.att + [self.eq, self.attp, self.drow, self.normwb,
                              self.sim] + self.maskqt):
            t.dead = True
        b.barrier()
        b.set_init_phase(3)
        self._ffn_phase(pool64)
        if self.stage >= 6:
            oy = self.out("y", [D, NT])
            for i in range(NC_F):
                self._dump_into(oy[i * 128:(i + 1) * 128, :], self.x[i])
        self.b.finalize()
        return self

    def _trig_tables(self, t_posB, t_k2, t_alt, invf, th, s1, s2):
        b = self.b
        si = self.si
        b.op("act", lambda s: s.activation(
            out=invf.ap, in_=t_k2.ap, func=AF.Exp,
            scale=-math.log(10000.0) / 128.0), reads=[t_k2], writes=[invf])
        b.op("dve", lambda v: v.tensor_scalar(
            out=th.ap, in0=t_posB.ap, scalar1=invf[:, 0:1], scalar2=None,
            op0=ALU.mult), reads=[t_posB, invf], writes=[th])

        def trig(dst, shift, mult_ap_tile, mult_imm):
            inv2pi = 1.0 / (2.0 * math.pi)
            b.op("dve", lambda v: v.tensor_scalar(
                out=s1.ap, in0=th.ap, scalar1=inv2pi, scalar2=shift * inv2pi,
                op0=ALU.mult, op1=ALU.add), reads=[th], writes=[s1])
            for hf in range(4):
                c = slice(hf * 256, (hf + 1) * 256)
                b.op("dve", (lambda c=c: lambda v: v.tensor_copy(
                    si.ap, s1[:, c]))(), reads=[s1], writes=[si])
                b.op("dve", (lambda c=c: lambda v: v.tensor_copy(
                    s1[:, c], si.ap))(), reads=[si], writes=[s1])
            b.op("dve", lambda v: v.scalar_tensor_tensor(
                out=s2.ap, in0=s1.ap, scalar=-2.0 * math.pi, in1=th.ap,
                op0=ALU.mult, op1=ALU.add), reads=[s1, th], writes=[s2])
            if shift != 0.0:
                b.op("dve", lambda v: v.tensor_single_scalar(
                    out=s2.ap, in_=s2.ap, scalar=shift, op=ALU.add),
                    reads=[s2], writes=[s2])
            b.op("dve", lambda v: v.tensor_single_scalar(
                out=s1.ap, in_=s2.ap, scalar=math.pi, op=ALU.is_gt),
                reads=[s2], writes=[s1])
            b.op("dve", lambda v: v.scalar_tensor_tensor(
                out=s2.ap, in0=s1.ap, scalar=-2.0 * math.pi, in1=s2.ap,
                op0=ALU.mult, op1=ALU.add), reads=[s1, s2], writes=[s2])
            b.op("dve", lambda v: v.tensor_single_scalar(
                out=s1.ap, in_=s2.ap, scalar=-math.pi, op=ALU.is_lt),
                reads=[s2], writes=[s1])
            b.op("dve", lambda v: v.scalar_tensor_tensor(
                out=s2.ap, in0=s1.ap, scalar=2.0 * math.pi, in1=s2.ap,
                op0=ALU.mult, op1=ALU.add), reads=[s1, s2], writes=[s2])
            b.op("dve", lambda v: v.tensor_scalar(
                out=s2.ap, in0=s2.ap, scalar1=math.pi, scalar2=-math.pi,
                op0=ALU.min, op1=ALU.max), reads=[s2], writes=[s2])
            b.op("act", lambda s: s.activation(
                out=s1.ap, in_=s2.ap, func=AF.Sin), reads=[s2], writes=[s1])
            if mult_ap_tile is not None:
                b.op("dve", lambda v: v.tensor_scalar(
                    out=dst.ap, in0=s1.ap, scalar1=mult_ap_tile[:, 0:1],
                    scalar2=None, op0=ALU.mult),
                    reads=[s1, mult_ap_tile], writes=[dst])
            else:
                b.op("dve", lambda v: v.tensor_single_scalar(
                    out=dst.ap, in_=s1.ap, scalar=mult_imm, op=ALU.mult),
                    reads=[s1], writes=[dst])

        s4 = float(DH) ** -0.25
        trig(self.sin_a, 0.0, t_alt, None)
        trig(self.cos_s, math.pi / 2.0, None, s4)

    def _layernorm(self, w, fuse_rms, xsrc, dst, tgt_dt=BF16):
        b = self.b
        sl = slice(w * W, (w + 1) * W)
        S0, S1, PA, PB, PA2, PB2 = self.banks[0:6]
        oc = self.onescol[:, 0:1]
        for i in range(NC_F):
            cb, sq = self.sbb[0], self.sbb[1]
            b.op("dve", (lambda i=i, cb=cb: lambda v: v.tensor_copy(
                cb.ap, xsrc[i][:, sl]))(), reads=[xsrc[i]], writes=[cb])
            b.op("act", (lambda i=i, sq=sq: lambda s: s.activation(
                out=sq.ap, in_=xsrc[i][:, sl], func=AF.Square))(),
                reads=[xsrc[i]], writes=[sq])
            b.mm(S0, S0.t[0:1, :], oc, cb.ap,
                 reads=[self.onescol, cb], start=(i == 0),
                 stop=(i == NC_F - 1), token=True)
            b.mm(S1, S1.t[0:1, :], oc, sq.ap,
                 reads=[self.onescol, sq], start=(i == 0),
                 stop=(i == NC_F - 1), token=True)
        r, rb = self.row, self.rowb
        mrow, erow = r[0:1, 0:512], r[0:1, 512:1024]
        msq = r[0:1, 1024:1536]
        varr, denr = r[0:1, 512:1024], r[0:1, 512:1024]
        alph, malph = r[0:1, 1024:1536], r[0:1, 512:1024]
        b.op("dve", lambda v: v.tensor_scalar(
            out=mrow, in0=S0.t[0:1, :], scalar1=1.0 / D, scalar2=None,
            op0=ALU.mult), reads=[S0], writes=[r])
        b.op("dve", lambda v: v.tensor_scalar(
            out=erow, in0=S1.t[0:1, :], scalar1=1.0 / D, scalar2=None,
            op0=ALU.mult), reads=[S1, r], writes=[r])
        b.op("dve", lambda v: v.tensor_tensor(
            out=msq, in0=mrow, in1=mrow, op=ALU.mult), reads=[r], writes=[r])
        b.op("dve", lambda v: v.tensor_tensor(
            out=varr, in0=erow, in1=msq, op=ALU.subtract),
            reads=[r], writes=[r])
        if fuse_rms:
            b.op("dve", lambda v: v.tensor_scalar(
                out=denr, in0=varr, scalar1=1.0 + 1e-6, scalar2=1e-11,
                op0=ALU.mult, op1=ALU.add), reads=[r], writes=[r])
        else:
            b.op("dve", lambda v: v.tensor_single_scalar(
                out=denr, in_=varr, scalar=1e-5, op=ALU.add),
                reads=[r], writes=[r])
        b.op("act", lambda s: s.activation(out=denr, in_=denr, func=AF.Ln),
             reads=[r], writes=[r])
        b.op("act", lambda s: s.activation(out=alph, in_=denr, func=AF.Exp,
             scale=-0.5), reads=[r], writes=[r])
        b.op("dve", lambda v: v.tensor_tensor(
            out=malph, in0=mrow, in1=alph, op=ALU.mult), reads=[r], writes=[r])
        arow, marow = rb[0:1, 0:512], rb[0:1, 512:1024]
        b.op("dve", lambda v: v.tensor_copy(arow, alph), reads=[r], writes=[rb])
        b.op("dve", lambda v: v.tensor_copy(marow, malph), reads=[r],
             writes=[rb])
        for i in range(NC_F):
            BA = PA if i % 2 == 0 else PA2
            BB = PB if i % 2 == 0 else PB2
            if fuse_rms:
                lh, lht = self.normwb[0:1, i * 128:(i + 1) * 128], self.normwb
            else:
                lh, lht = self.ones2[0:1, :], self.ones2
            b.mm(BA, BA.ap, lh, arow, reads=[lht, self.rowb],
                 start=True, stop=True)
            b.mm(BB, BB.ap, lh, marow, reads=[lht, self.rowb],
                 start=True, stop=True)
            b.op("dve", (lambda i=i, BA=BA: lambda v: v.tensor_tensor(
                out=self.lnscr.ap, in0=xsrc[i][:, sl], in1=BA.ap,
                op=ALU.mult))(), reads=[xsrc[i], BA], writes=[self.lnscr])
            b.op("dve", (lambda i=i, BB=BB: lambda v: v.tensor_tensor(
                out=dst[i].ap, in0=self.lnscr.ap, in1=BB.ap,
                op=ALU.subtract))(), reads=[self.lnscr, BB], writes=[dst[i]])

    def _qkv_rope(self, w):
        b, I = self.b, self.I
        sl = slice(w * W, (w + 1) * W)
        BQ = [self.banks[6], self.banks[7]]
        WB = 32768
        for jp in range(16):
            pa = (jp % 2) * 2
            pair_ap = self.ab.t[:, WB + pa * 2048:WB + (pa + 2) * 2048]
            b.dma("pool", pair_ap.rearrange("p (n j) -> p n j", n=2),
                  I["wqk"][2 * jp:2 * jp + 2].rearrange("n p j -> p n j"),
                  writes=[self.wbuf[pa], self.wbuf[pa + 1]])
            for jl in range(2):
                jc = 2 * jp + jl
                buf = self.wbuf[pa + jl]
                bank = BQ[jc % 2]
                for kc in range(NC_F):
                    b.mm(bank, bank.ap, buf[:, kc * 128:(kc + 1) * 128],
                         self.xn[kc].ap,
                         reads=[buf, self.xn[kc]] if kc == 0
                         else [self.xn[kc]],
                         start=(kc == 0), stop=(kc == NC_F - 1))
                dst = self.qh[jc] if jc < 16 else self.kh[jc - 16]
                self._rope(bank, dst, sl)
        BV = self.banks[0:4]
        WB = 32768
        for jb in range(4):
            for dh8 in range(2):
                pa = ((jb * 2 + dh8) % 2) * 2
                pair_ap = self.ab.t[:, WB + pa * 2048:WB + (pa + 2) * 2048]
                wts = [self.wbuf[pa], self.wbuf[pa + 1]]
                b.dma("pool", pair_ap.rearrange("p (g j) -> p g j", g=8),
                      I["wv"][dh8 * 8:(dh8 + 1) * 8, :,
                              jb * 512:(jb + 1) * 512]
                      .rearrange("g p j -> p g j"), writes=wts)
                for dcl in range(8):
                    dc = dh8 * 8 + dcl
                    vb = wts[dcl // 4]
                    for tc in range(4):
                        b.mm(BV[tc], BV[tc].ap,
                             self.xn[dc][:, tc * 128:(tc + 1) * 128],
                             vb[:, (dcl % 4) * 512:(dcl % 4 + 1) * 512],
                             reads=[self.xn[dc], vb],
                             start=(dc == 0), stop=(dc == 15),
                             token=(dcl % 4 == 3 and tc == 3))
            for tc in range(4):
                b.op("dve", (lambda tc=tc, jb=jb: lambda v: v.tensor_copy(
                    self.vtm[tc][:, jb * 512:(jb + 1) * 512], BV[tc].ap))(),
                    reads=[BV[tc]], writes=[self.vtm[tc]])

    def _rope(self, bank, dst, sl):
        b = self.b
        r0, r1 = self.rscr[0], self.lnscr
        swapmask = [(i // 2) * 2 + (1 - (i % 2)) for i in range(32)]
        b.op("dve", lambda v: v.stream_shuffle(r0.ap, bank.ap, mask=swapmask),
             reads=[bank], writes=[r0, self.sim1ov])
        b.op("dve", (lambda sl=sl: lambda v: v.tensor_tensor(
            out=r1.ap, in0=bank.ap, in1=self.cos_s[:, sl], op=ALU.mult))(),
            reads=[bank, self.cos_s], writes=[r1])
        b.op("dve", (lambda sl=sl: lambda v: v.tensor_tensor(
            out=r0.ap, in0=r0.ap, in1=self.sin_a[:, sl], op=ALU.mult))(),
            reads=[r0, self.sin_a], writes=[r0, self.sim1ov])
        b.op("dve", (lambda dst=dst: lambda v: v.tensor_tensor(
            out=dst.ap, in0=r1.ap, in1=r0.ap, op=ALU.add))(),
            reads=[r0, r1], writes=[dst])

    def _attn_qt(self, w, h):
        b = self.b
        par = h % 2
        q, k = self.qh[h], self.kh[h]
        SA = self.banks[par]
        SB = self.banks[2 + par]
        sim = self.sim if par == 0 else self.sim1
        NM, RS, ND = self.negmax[par], self.rsum[par], self.negD[par]
        LC, H32, HL = self.locol[par], self.hi32[par], self.hilo[par]
        for qc in range(4):
            nmx = NM[:, qc:qc + 1]
            rs = RS[:, qc:qc + 1]
            qsl = q[:, qc * 128:(qc + 1) * 128]
            wn = 128 * (qc + 1)
            if qc < 3:
                tot = NP + wn
                b.mm(SA, SA.t[:, 0:NP], qsl, self.pmk[h].ap,
                     reads=[q, self.pmk[h]], start=True, stop=False)
                b.mm(SA, SA.t[:, NP:tot], qsl, k[:, 0:wn],
                     reads=[k], start=False, stop=True)
                b.op("dve", (lambda qc=qc, sim=sim, SA=SA, tot=tot:
                     lambda v: v.tensor_tensor(
                         out=sim[:, 0:tot], in0=SA.t[:, 0:tot],
                         in1=self.maskqt[qc][:, 0:tot], op=ALU.add))(),
                    reads=[SA, self.maskqt[qc]], writes=[sim])
            else:
                tot = KT
                b.mm(SA, SA.ap, qsl, k.ap, reads=[q, k],
                     start=True, stop=True)
                b.mm(SB, SB.t[:, 0:NP], qsl, self.pmk[h].ap,
                     reads=[q, self.pmk[h]], start=True, stop=True)
                b.op("dve", (lambda sim=sim, SB=SB: lambda v: v.tensor_copy(
                    sim[:, 0:NP], SB.t[:, 0:NP]))(),
                    reads=[SB], writes=[sim])
                b.op("dve", (lambda qc=qc, sim=sim, SA=SA:
                     lambda v: v.tensor_tensor(
                         out=sim[:, NP:], in0=SA.ap,
                         in1=self.maskqt[qc][:, NP:], op=ALU.add))(),
                    reads=[SA, self.maskqt[qc]], writes=[sim])
            b.op("dve", (lambda sim=sim, nmx=nmx, tot=tot:
                 lambda v: v.reduce_max(
                     out=nmx, in_=sim[:, 0:tot], axis=AX.X, negate=True))(),
                reads=[sim], writes=[NM])
            b.op("act", (lambda sim=sim, nmx=nmx, rs=rs, tot=tot:
                 lambda s: s.activation(
                     out=self.eq[:, 0:tot], in_=sim[:, 0:tot], func=AF.Exp,
                     bias=nmx, accum_out=rs))(), reads=[sim, NM],
                writes=[self.eq, RS])
        b.op("act", (lambda RS=RS: lambda s: s.activation(
            out=RS.ap, in_=RS.ap, func=AF.Ln))(),
            reads=[RS], writes=[RS])
        b.op("dve", (lambda NM=NM, RS=RS, ND=ND: lambda v: v.tensor_tensor(
            out=ND.ap, in0=NM.ap, in1=RS.ap, op=ALU.subtract))(),
            reads=[NM, RS], writes=[ND])
        hie = HL.t.rearrange("p (a b) -> p a b", b=2)[:, :, 0]
        hio = HL.t.rearrange("p (a b) -> p a b", b=2)[:, :, 1]
        b.op("dve", (lambda hie=hie, ND=ND, HL=HL: lambda v: v.tensor_copy(
            hie, ND.ap))(), reads=[ND], writes=[HL])
        b.op("dve", (lambda hie=hie, H32=H32, HL=HL: lambda v: v.tensor_copy(
            H32.ap, hie))(), reads=[HL], writes=[H32])
        b.op("dve", (lambda ND=ND, H32=H32, LC=LC: lambda v: v.tensor_tensor(
            out=LC.ap, in0=ND.ap, in1=H32.ap, op=ALU.subtract))(),
            reads=[ND, H32], writes=[LC])
        b.op("dve", (lambda hio=hio, LC=LC, HL=HL: lambda v: v.tensor_copy(
            hio, LC.ap))(), reads=[LC], writes=[HL])

    def _attn_kt(self, w, h):
        b = self.b
        par = h % 2
        q, k = self.qh[h], self.kh[h]
        TD = self.banks[4]
        SK = [self.banks[5], self.banks[6]]
        AV = self.banks[7]
        HL = self.hilo[par]
        for qc in range(4):
            b.mm(TD, TD.t[0:2, qc * 128:(qc + 1) * 128],
                 HL[:, 2 * qc:2 * qc + 2],
                 self.identb.ap,
                 reads=[HL, self.identb],
                 start=(qc == 0), stop=(qc == 3))
        b.op("dve", lambda v: v.tensor_copy(self.drow.ap, TD.t[0:2, :]),
             reads=[TD], writes=[self.drow])
        for c in range(5):
            bank = SK[c % 2]
            qt0 = 0 if c == 0 else 128 * (c - 1)
            n = W - qt0
            if c == 0:
                lh_k, lht, cols, dst = self.pmk[h].ap, self.pmk[h], NP, self.attp
            else:
                lh_k = k[:, (c - 1) * 128:c * 128]
                lht, cols, dst = k, 128, self.att[c - 1]
            dstp = bank.t[0:cols, 0:n]
            b.mm(bank, dstp, self.ones2[0:2, 0:cols], self.drow[0:2, qt0:W],
                 reads=[self.ones2, self.drow], start=True, stop=False)
            b.mm(bank, dstp, lh_k, q[:, qt0:W], reads=[lht, q],
                 start=False, stop=True)
            b.op("act", (lambda dstp=dstp, dst=dst, qt0=qt0:
                 lambda s: s.activation(
                     out=dst[:, qt0:W] if qt0 else dst.ap, in_=dstp,
                     func=AF.Exp))(),
                reads=[bank], writes=[dst])
            if c >= 1:
                b.op("dve", (lambda c=c, dst=dst, qt0=qt0:
                     lambda v: v.tensor_tensor(
                         out=dst[:, qt0:W], in0=dst[:, qt0:W],
                         in1=self.mask01[c - 1][:, qt0:W],
                         op=ALU.mult))(), reads=[dst, self.mask01[c - 1]],
                    writes=[dst])
        b.mm(AV, AV.ap, self.pmv[h].ap, self.attp.ap,
             reads=[self.pmv[h], self.attp], start=True, stop=False)
        for c in range(1, 5):
            qt0 = 128 * (c - 1)
            b.mm(AV, AV.t[:, qt0:W],
                 self.vtm[c - 1][:, h * 128:(h + 1) * 128],
                 self.att[c - 1][:, qt0:W],
                 reads=[self.vtm[c - 1], self.att[c - 1]],
                 start=False, stop=(c == 4))
        b.op("dve", (lambda h=h: lambda v: v.tensor_copy(
            self.attnout[h].ap, AV.ap))(), reads=[AV],
            writes=[self.attnout[h]])
        if self.debug and self.stage == 31 and w == 0 and h == 0:
            self._dump("dbg_negmax", self.negmax)
            self._dump("dbg_rsum", self.rsum)
            self._dump("dbg_negD", self.negD)
            self._dump("dbg_hilo", self.hilo)
            self._dump("dbg_drow", self.drow)

    def _outproj(self, w):
        b, I = self.b, self.I
        sl = slice(w * W, (w + 1) * W)
        BO = [self.banks[0], self.banks[1]]
        WB = 32768
        for oc in range(NC_F):
            buf = self.wbuf[(oc % 4) if False else (oc // 2 % 2) * 2 + oc % 2]
            if oc % 2 == 0:
                pa = (oc // 2 % 2) * 2
                pair_ap = self.ab.t[:, WB + pa * 2048:WB + (pa + 2) * 2048]
                b.dma("pool", pair_ap.rearrange("p (n j) -> p n j", n=2),
                      I["wo"][oc:oc + 2].rearrange("n p j -> p n j"),
                      writes=[self.wbuf[pa], self.wbuf[pa + 1]])
            bank = BO[oc % 2]
            for kc in range(NC_F):
                b.mm(bank, bank.ap, buf[:, kc * 128:(kc + 1) * 128],
                     self.attnout[kc].ap,
                     reads=[buf, self.attnout[kc]] if kc == 0
                     else [self.attnout[kc]],
                     start=(kc == 0), stop=(kc == NC_F - 1))
            b.op("dve", (lambda oc=oc, bank=bank: lambda v: v.tensor_tensor(
                out=self.x[oc][:, sl], in0=self.x[oc][:, sl], in1=bank.ap,
                op=ALU.add))(), reads=[bank, self.x[oc]], writes=[self.x[oc]])

    def _ffn_phase(self, pool64):
        b, I = self.b, self.I
        hp = [[Tile(self.ab.t[:, 512 * (8 * e + hc):512 * (8 * e + hc) + 512],
                    f"hp{e}_{hc}") for hc in range(8)] for e in range(NE)]
        self.lgf = Tile(self.af.t[0:8, 2048:2560], "lgf")
        self.g64 = Tile(self.af.t[:, 2560:2592], "g64")
        self.gA = Tile(self.af.t[:, 2592:2624], "gA")
        self.gC = Tile(self.af.t[:, 2624:2656], "gC")
        self.gm4 = Tile(self.af.t[:, 2656:2676], "gm4")
        b.dma("pool", self.wg1.ap.rearrange("p (c e) -> p c e", c=16),
              I["wg1T"].rearrange("c p e -> p c e"), writes=[self.wg1])
        b.dma("pool", self.wg2.ap, I["wg2T"][:], writes=[self.wg2])
        b.dma("sync", self.selb.ap, I["sel8"][:], writes=[self.selb])
        for half in range(2):
            self._layernorm_cols(half, xsrc=self.x, dst_tiles=self.xth)
            self._gate(half)
            if self.debug and self.stage == 45 and half == 0:
                o = self.out("dbg_xt", [NC_F, 128, W], BF16)
                for i in range(NC_F):
                    self._dump_into(o[i], self.xth[i])
                return
            if self.debug and self.stage in (5, 51) and half == (1 if self.stage == 5 else 0):
                self._dump("dbg_scores", self.score_em)
                return
            self._ffn_half(half, hp)

    def _layernorm_cols(self, w, xsrc, dst_tiles):
        b = self.b
        sl = slice(w * W, (w + 1) * W)
        S0, S1, PA, PB, PA2, PB2 = self.banks[0:6]
        oc = self.onescol[:, 0:1]
        for i in range(NC_F):
            cb, sq = self.sbb[0], self.sbb[1]
            b.op("dve", (lambda i=i, cb=cb: lambda v: v.tensor_copy(
                cb.ap, xsrc[i][:, sl]))(), reads=[xsrc[i]], writes=[cb])
            b.op("act", (lambda i=i, sq=sq: lambda s: s.activation(
                out=sq.ap, in_=xsrc[i][:, sl], func=AF.Square))(),
                reads=[xsrc[i]], writes=[sq])
            b.mm(S0, S0.t[0:1, :], oc, cb.ap, reads=[self.onescol, cb],
                 start=(i == 0), stop=(i == NC_F - 1), token=True)
            b.mm(S1, S1.t[0:1, :], oc, sq.ap, reads=[self.onescol, sq],
                 start=(i == 0), stop=(i == NC_F - 1), token=True)
        r, rb = self.row, self.rowb
        mrow, erow = r[0:1, 0:512], r[0:1, 512:1024]
        msq = r[0:1, 1024:1536]
        varr, denr = r[0:1, 512:1024], r[0:1, 512:1024]
        alph, malph = r[0:1, 1024:1536], r[0:1, 512:1024]
        b.op("dve", lambda v: v.tensor_scalar(
            out=mrow, in0=S0.t[0:1, :], scalar1=1.0 / D, scalar2=None,
            op0=ALU.mult), reads=[S0], writes=[r])
        b.op("dve", lambda v: v.tensor_scalar(
            out=erow, in0=S1.t[0:1, :], scalar1=1.0 / D, scalar2=None,
            op0=ALU.mult), reads=[S1, r], writes=[r])
        b.op("dve", lambda v: v.tensor_tensor(
            out=msq, in0=mrow, in1=mrow, op=ALU.mult), reads=[r], writes=[r])
        b.op("dve", lambda v: v.tensor_tensor(
            out=varr, in0=erow, in1=msq, op=ALU.subtract),
            reads=[r], writes=[r])
        b.op("dve", lambda v: v.tensor_single_scalar(
            out=denr, in_=varr, scalar=1e-5, op=ALU.add), reads=[r], writes=[r])
        b.op("act", lambda s: s.activation(out=denr, in_=denr, func=AF.Ln),
             reads=[r], writes=[r])
        b.op("act", lambda s: s.activation(out=alph, in_=denr, func=AF.Exp,
             scale=-0.5), reads=[r], writes=[r])
        b.op("dve", lambda v: v.tensor_tensor(
            out=malph, in0=mrow, in1=alph, op=ALU.mult), reads=[r], writes=[r])
        arow, marow = rb[0:1, 0:512], rb[0:1, 512:1024]
        b.op("dve", lambda v: v.tensor_copy(arow, alph), reads=[r], writes=[rb])
        b.op("dve", lambda v: v.tensor_copy(marow, malph), reads=[r],
             writes=[rb])
        for i in range(NC_F):
            BA = PA if i % 2 == 0 else PA2
            BB = PB if i % 2 == 0 else PB2
            lh, lht = self.ones2[0:1, :], self.ones2
            b.mm(BA, BA.ap, lh, arow, reads=[lht, self.rowb],
                 start=True, stop=True)
            b.mm(BB, BB.ap, lh, marow, reads=[lht, self.rowb],
                 start=True, stop=True)
            b.op("dve", (lambda i=i, BA=BA: lambda v: v.tensor_tensor(
                out=self.lnscr.ap, in0=xsrc[i][:, sl], in1=BA.ap,
                op=ALU.mult))(), reads=[xsrc[i], BA], writes=[self.lnscr])
            b.op("dve", (lambda i=i, BB=BB: lambda v: v.tensor_tensor(
                out=dst_tiles[i].ap, in0=self.lnscr.ap, in1=BB.ap,
                op=ALU.subtract))(), reads=[self.lnscr, BB],
                writes=[dst_tiles[i]])

    def _gate(self, half):
        b, I = self.b, self.I
        hsl = slice(half * 512, (half + 1) * 512)
        G1, G2 = self.banks[6], self.banks[7]
        for kc in range(NC_F):
            b.mm(G1, G1.t[0:8, :], self.wg1[:, kc * 8:(kc + 1) * 8],
                 self.xth[kc].ap, reads=[self.wg1, self.xth[kc]],
                 start=(kc == 0), stop=(kc == NC_F - 1))
        b.op("act", lambda s: s.activation(
            out=self.score_em[0:8, hsl], in_=G1.t[0:8, :], func=AF.Tanh),
            reads=[G1], writes=[self.score_em])
        b.mm(G2, G2.t[0:8, :], self.wg2.ap, self.score_em[0:8, hsl],
             reads=[self.wg2, self.score_em], start=True, stop=True)
        b.op("dve", lambda v: v.tensor_copy(
            self.lgf.ap, G2.t[0:8, :]), reads=[G2],
            writes=[self.lgf])
        if self.debug and self.stage == 51 and half == 0:
            self._dump("dbg_logits", self.lgf)
        TS = self.banks[6]
        for tc in range(4):
            b.mm(TS, TS.t[0:128, 8 * tc:8 * tc + 8],
                 self.lgf[0:8, tc * 128:(tc + 1) * 128], self.identf.ap,
                 reads=[self.lgf, self.identf],
                 start=(tc == 0), stop=(tc == 3))
        g64, gA, gC, gm4 = self.g64, self.gA, self.gC, self.gm4
        m1, m2 = gm4[:, 0:4], gm4[:, 4:8]
        s1, s2 = gm4[:, 8:12], gm4[:, 12:16]

        def e3(ap):
            return ap.rearrange("p (t e) -> p t e", e=8)

        def b3(ap):
            return ap.rearrange("p (t o) -> p t o", o=1).broadcast_to((128, 4, 8))

        b.op("dve", lambda v: v.tensor_copy(g64.ap, TS.t[0:128, 0:32]),
             reads=[TS], writes=[g64])
        b.op("dve", lambda v: v.tensor_reduce(
            out=m1, in_=e3(g64.ap), axis=AX.X, op=ALU.max, negate=True),
            reads=[g64], writes=[gm4])
        b.op("dve", lambda v: v.tensor_tensor(
            out=e3(gA.ap), in0=e3(g64.ap), in1=b3(m1), op=ALU.add),
            reads=[g64, gm4], writes=[gA])
        b.op("dve", lambda v: v.tensor_single_scalar(
            out=gA.ap, in_=gA.ap, scalar=0.0, op=ALU.is_ge),
            reads=[gA], writes=[gA])
        b.op("dve", lambda v: v.scalar_tensor_tensor(
            out=g64.ap, in0=gA.ap, scalar=-1e30, in1=g64.ap,
            op0=ALU.mult, op1=ALU.add), reads=[gA, g64], writes=[g64])
        b.op("dve", lambda v: v.tensor_reduce(
            out=m2, in_=e3(g64.ap), axis=AX.X, op=ALU.max, negate=True),
            reads=[g64], writes=[gm4])
        b.op("dve", lambda v: v.tensor_tensor(
            out=e3(gC.ap), in0=e3(g64.ap), in1=b3(m2), op=ALU.add),
            reads=[g64, gm4], writes=[gC])
        b.op("dve", lambda v: v.tensor_single_scalar(
            out=gC.ap, in_=gC.ap, scalar=0.0, op=ALU.is_ge),
            reads=[gC], writes=[gC])
        b.op("dve", lambda v: v.tensor_tensor(
            out=s1, in0=m2, in1=m1, op=ALU.subtract),
            reads=[gm4], writes=[gm4])
        b.op("act", lambda s: s.activation(out=s1, in_=s1, func=AF.Sigmoid),
             reads=[gm4], writes=[gm4])
        b.op("dve", lambda v: v.tensor_scalar(
            out=s2, in0=s1, scalar1=-1.0, scalar2=1.0,
            op0=ALU.mult, op1=ALU.add), reads=[gm4], writes=[gm4])
        b.op("dve", lambda v: v.tensor_tensor(
            out=e3(gA.ap), in0=e3(gA.ap), in1=b3(s1), op=ALU.mult),
            reads=[gA, gm4], writes=[gA])
        b.op("dve", lambda v: v.tensor_tensor(
            out=e3(gC.ap), in0=e3(gC.ap), in1=b3(s2), op=ALU.mult),
            reads=[gC, gm4], writes=[gC])
        b.op("dve", lambda v: v.tensor_tensor(
            out=gA.ap, in0=gA.ap, in1=gC.ap, op=ALU.add),
            reads=[gA, gC], writes=[gA])
        b.op("dve", lambda v: v.tensor_copy(self.sbb[0][:, 0:32], gA.ap),
             reads=[gA], writes=[self.sbb[0]])
        G2b = self.banks[7]
        for tc in range(4):
            tsl = slice(half * 512 + tc * 128, half * 512 + (tc + 1) * 128)
            b.mm(G2b, G2b.t[0:8, 0:128],
                 self.sbb[0][:, 8 * tc:8 * tc + 8],
                 self.identb[:, 0:128], reads=[self.sbb[0], self.identb],
                 start=True, stop=True)
            b.op("dve", (lambda tsl=tsl, G2b=G2b: lambda v: v.tensor_copy(
                self.score_em[0:8, tsl], G2b.t[0:8, 0:128]))(),
                reads=[G2b], writes=[self.score_em])

    def _ffn_half(self, half, hp):
        b, I = self.b, self.I
        hsl = slice(half * 512, (half + 1) * 512)
        scoreB = self.rscr[0]
        silu2 = [self.rscr[1], self.lnscr]
        wb = self.wbuf
        WB = 32768
        UPP = [self.banks[6], self.banks[7]]
        SCB = self.banks[5]
        for e in range(NE):
            b.mm(SCB, SCB.ap, self.selb[0:8, e * 128:(e + 1) * 128],
                 self.score_em[0:8, hsl], reads=[self.selb, self.score_em],
                 start=True, stop=True)
            b.op("dve", lambda v: v.tensor_copy(
                scoreB.ap, SCB.ap), reads=[SCB], writes=[scoreB])
            for hc in range(8):
                if hc % 2 == 0:
                    pa = (hc // 2 % 2) * 2
                    pair_ap = self.ab.t[:, WB + pa * 2048:WB + (pa + 2) * 2048]
                    b.dma("pool", pair_ap.rearrange("p (n j) -> p n j", n=2),
                          I["wupT"][e, hc:hc + 2].rearrange("n p j -> p n j"),
                          writes=[wb[pa], wb[pa + 1]])
                buf = wb[(hc // 2 % 2) * 2 + hc % 2]
                bc = self.bupcol[hc % 2]
                b.dma("sync", bc.ap, I["bup"][e, hc], writes=[bc])
                UP = UPP[hc % 2]
                sscr = silu2[hc % 2]
                for kc in range(NC_F):
                    b.mm(UP, UP.ap, buf[:, kc * 128:(kc + 1) * 128],
                         self.xth[kc].ap,
                         reads=[buf, self.xth[kc]] if kc == 0
                         else [self.xth[kc]],
                         start=(kc == 0), stop=(kc == NC_F - 1))
                b.op("act", (lambda bc=bc, UP=UP, sscr=sscr:
                     lambda s: s.activation(
                         out=sscr.ap, in_=UP.ap, func=AF.Silu,
                         bias=bc[:, 0:1]))(), reads=[UP, bc],
                    writes=[sscr])
                b.op("dve", (lambda e=e, hc=hc, sscr=sscr:
                     lambda v: v.tensor_tensor(
                         out=hp[e][hc].ap, in0=sscr.ap,
                         in1=scoreB.ap, op=ALU.mult))(),
                    reads=[sscr, scoreB], writes=[hp[e][hc]])
        for og in range(2):
            DB = self.banks
            b.dma("pool", self.bdn.ap,
                  I["bdown"][0:8, og * 1024:(og + 1) * 1024],
                  writes=[self.bdn])
            for ocl in range(8):
                b.mm(DB[ocl], DB[ocl].ap,
                     self.bdn[0:8, ocl * 128:(ocl + 1) * 128],
                     self.score_em[0:8, hsl],
                     reads=[self.bdn, self.score_em],
                     start=True, stop=False)
            for e in range(NE):
                for hc in range(8):
                    if hc % 4 == 0:
                        pa = (hc // 4 % 2) * 2
                        pair_ap = self.ab.t[:, WB + pa * 2048:
                                            WB + (pa + 2) * 2048]
                        b.dma("pool",
                              pair_ap.rearrange("p (n j) -> p n j", n=4),
                              I["wdT"][e, hc:hc + 4, :,
                                       og * 1024:(og + 1) * 1024]
                              .rearrange("n p j -> p n j"),
                              writes=[wb[pa], wb[pa + 1]])
                    buf = wb[(hc // 4 % 2) * 2 + (hc % 4) // 2]
                    boff = (hc % 2) * 1024
                    for ocl in range(8):
                        last = (e == NE - 1 and hc == 7)
                        b.mm(DB[ocl], DB[ocl].ap,
                             buf[:, boff + ocl * 128:boff + (ocl + 1) * 128],
                             hp[e][hc].ap,
                             reads=[buf, hp[e][hc]] if ocl in (0, 7)
                             else [hp[e][hc]],
                             start=False, stop=last,
                             token=(ocl == 7))
            for ocl in range(8):
                oc = og * 8 + ocl
                b.op("dve", (lambda oc=oc, ocl=ocl, hsl=hsl, DB=DB:
                     lambda v: v.tensor_tensor(
                         out=self.x[oc][:, hsl], in0=self.x[oc][:, hsl],
                         in1=DB[ocl].ap, op=ALU.add))(),
                    reads=[DB[ocl], self.x[oc]], writes=[self.x[oc]])


from concourse.bass_utils import run_bass_kernel_spmd


def kernel(**inputs):
    mk = MK(stage=6, debug=False).build()
    shared = {}
    in_maps = [host_inputs(inputs["L0_attn_x"], inputs, c, shared=shared)
               for c in range(8)]
    res = run_bass_kernel_spmd(mk.nc, in_maps, core_ids=list(range(8)))
    ys = [np.asarray(r["y"]).T for r in res.results]
    return np.concatenate(ys, axis=0).reshape(2, 4096, 2048).astype(np.float32)


# version 10
# speedup vs baseline: 1.0901x; 1.0109x over previous
import math
import contextlib
import numpy as np

import concourse.bass as bass
import concourse.mybir as mybir

F32 = mybir.dt.float32
BF16 = mybir.dt.bfloat16
I32 = mybir.dt.int32
AF = mybir.ActivationFunctionType
ALU = mybir.AluOpType
AX = mybir.AxisListType

D = 2048
NT = 1024
W = 512
NH = 16
DH = 128
NP = 16
KT = NP + W
NC_F = D // 128
HE = 1024
NE = 8
NEG = -np.float32(3.4028235e38)

ENGS = ("pe", "dve", "act", "pool", "sync")
PROMOTE = ("d_init", "d_initp", "d_init2", "d_initp2", "d_init3", "d_initp3")


class Tile:
    __slots__ = ("t", "name", "last_w", "readers", "pending", "dead", "dsem")

    def __init__(self, t, name):
        self.t = t
        self.name = name
        self.last_w = None
        self.readers = []
        self.pending = []
        self.dead = False
        self.dsem = None

    def __getitem__(self, idx):
        return self.t[idx]

    @property
    def ap(self):
        import bass_rust
        if isinstance(self.t, bass_rust.AP):
            return self.t
        return self.t[:]


class Builder:
    def __init__(self, nc):
        self.nc = nc
        self.ops = {e: [] for e in ENGS}
        import collections
        self.cnt = collections.defaultdict(int)
        for s in ("pe", "dve", "act", "pool"):
            self.cnt[s] = 0
        self.nops = 0
        self.init_sem = {"sync": "d_init", "pool": "d_initp"}
        self.barrier_pending = {}

    def set_init_phase(self, n):
        self.init_sem = {"sync": f"d_init{n}", "pool": f"d_initp{n}"}

    def barrier(self):
        snap = [(s, v) for s, v in self.cnt.items() if v > 0]
        for e in ENGS:
            self.barrier_pending[e] = list(snap)

    def _take_barrier(self, eng):
        w = self.barrier_pending.pop(eng, [])
        return w

    def op(self, eng, fn, reads=(), writes=(), extra=(), dma=False):
        waits = self._take_barrier(eng) + list(extra)
        for t in reads:
            assert not t.dead, t.name
            if t.last_w is not None:
                waits.append(t.last_w)
        for t in writes:
            assert not t.dead, t.name
            if t.last_w is not None:
                waits.append(t.last_w)
            waits.extend(t.readers)
        if dma:
            sem = dma if isinstance(dma, str) else self.init_sem[eng]
            inc = 16
        else:
            sem, inc = eng, 1
        self.cnt[sem] += inc
        tok = (sem, self.cnt[sem])
        self.ops[eng].append((fn, tuple(waits), (sem, inc)))
        self.nops += 1
        for t in reads:
            t.readers.append(tok)
        for t in writes:
            t.last_w = tok
            t.readers = [tok]
        return tok

    def mm(self, bank, out_ap, lh, rh, reads=(), start=False, stop=False,
           token=False, extra=()):
        waits = self._take_barrier("pe") + list(extra)
        if start:
            if bank.last_w is not None:
                waits.append(bank.last_w)
            waits.extend(bank.readers)
            bank.pending = []
        for t in reads:
            assert not t.dead, t.name
            if t.last_w is not None:
                waits.append(t.last_w)
            bank.pending.append(t)
        tok, inc = None, None
        if stop or token:
            self.cnt["pe"] += 1
            tok = ("pe", self.cnt["pe"])
            inc = ("pe", 1)
        self.ops["pe"].append((
            (lambda o=out_ap, a=lh, b=rh, s=start, p=stop: (
                lambda t: t.matmul(o, a, b, start=s, stop=p)))(),
            tuple(waits), inc))
        self.nops += 1
        if tok is not None:
            for t in bank.pending:
                t.readers.append(tok)
            bank.pending = []
        if stop:
            bank.last_w = tok
            bank.readers = [tok]
        return tok

    def mmg(self, bank, mms, reads=()):
        n = len(mms)
        tok = None
        for i, (o, lh, rh) in enumerate(mms):
            tok = self.mm(bank, o, lh, rh, reads=reads if i == 0 else (),
                          start=(i == 0), stop=(i == n - 1))
        return tok

    def dma(self, eng, out_ap, in_ap, reads=(), writes=(), extra=()):
        sem = True
        for t in writes:
            if t.dsem:
                sem = t.dsem
        return self.op(eng, lambda e: e.dma_start(out=out_ap, in_=in_ap),
                       reads=reads, writes=writes, extra=extra, dma=sem)

    def dma_out(self, out_ap, in_ap, reads=()):
        return self.op("sync", lambda e: e.dma_start(out=out_ap, in_=in_ap),
                       reads=reads, dma="d_out")

    def finalize(self):
        nc = self.nc
        finals = dict(self.cnt)
        with contextlib.ExitStack() as st:
            sems = {s: st.enter_context(nc.semaphore(s))
                    for s in self.cnt if self.cnt[s] > 0}
            blk = st.enter_context(nc.Block())

            def emit(eng_obj, stream, last_wait=None):
                done = {}
                for fn, waits, inc in stream:
                    byname = {}
                    for (s, v) in waits:
                        if s in PROMOTE:
                            v = finals[s]
                        if done.get(s, 0) < v and byname.get(s, 0) < v:
                            byname[s] = v
                    for s, v in sorted(byname.items()):
                        eng_obj.wait_ge(sems[s], v)
                        done[s] = v
                    inst = fn(eng_obj)
                    if inc is not None:
                        s, amt = inc
                        inst.then_inc(sems[s], amt)
                if last_wait is not None and finals.get(last_wait, 0) > 0:
                    eng_obj.wait_ge(sems[last_wait], finals[last_wait])

            @blk.tensor
            def _(t):
                emit(t, self.ops["pe"])

            @blk.vector
            def _(v):
                emit(v, self.ops["dve"])

            @blk.scalar
            def _(s):
                emit(s, self.ops["act"])

            @blk.gpsimd
            def _(g):
                emit(g, self.ops["pool"])

            @blk.sync
            def _(s):
                emit(s, self.ops["sync"], last_wait="d_out")


def host_inputs(x_full, wd, core, shared=None):
    import ml_dtypes
    c = core
    if shared is not None and shared:
        m = dict(shared)
        xt = np.ascontiguousarray(
            np.asarray(x_full).reshape(8192, D)[c * NT:(c + 1) * NT, :].T)
        rows = np.arange(c * NT, (c + 1) * NT)
        pos = (rows % 4096).astype(np.float32)
        m["xT"] = xt
        m["posB"] = np.ascontiguousarray(
            np.broadcast_to(pos[None, :], (128, NT)))
        return m
    xt = np.ascontiguousarray(
        np.asarray(x_full).reshape(8192, D)[c * NT:(c + 1) * NT, :].T)
    rows = np.arange(c * NT, (c + 1) * NT)
    pos = (rows % 4096).astype(np.float32)
    posB = np.ascontiguousarray(np.broadcast_to(pos[None, :], (128, NT)))
    p = np.arange(128)
    k2col = (2 * (p // 2)).astype(np.float32)[:, None]
    s4 = float(DH) ** -0.25
    altcol = (np.where(p % 2 == 1, 1.0, -1.0) * s4).astype(np.float32)[:, None]
    ident = np.eye(128, dtype=ml_dtypes.bfloat16)
    identf = np.eye(8, dtype=np.float32)
    ones2 = np.ones((2, 128), dtype=ml_dtypes.bfloat16)
    sel8 = np.zeros((8, 8 * 128), dtype=ml_dtypes.bfloat16)
    for e in range(8):
        sel8[e, e * 128:(e + 1) * 128] = 1.0
    qt = np.arange(W)[:, None]
    ktw = np.arange(W)[None, :]
    allowed_w = (ktw < 16) | ((qt >= 16) & (qt >= ktw))
    maskqt = np.zeros((W, KT), dtype=np.float32)
    maskqt[:, NP:] = np.where(allowed_w, 0.0, NEG)
    maskqt = np.ascontiguousarray(maskqt.reshape(4, 128, KT))
    m01 = allowed_w.T.astype(ml_dtypes.bfloat16)
    mask01T = np.ascontiguousarray(np.asarray(m01).reshape(4, 128, W))

    wqkvT = np.ascontiguousarray(np.asarray(wd["L0_attn_w_qkv"]).T)
    qk = np.ascontiguousarray(
        wqkvT[:, :4096].reshape(16, 128, 32, 128)
        .transpose(2, 1, 0, 3).reshape(32, 128, 2048))
    wv = np.ascontiguousarray(wqkvT[:, 4096:].reshape(16, 128, 2048))
    woT = np.ascontiguousarray(np.asarray(wd["L0_attn_w_out"]).T)
    wo = np.ascontiguousarray(
        woT.reshape(16, 128, 16, 128).transpose(2, 1, 0, 3).reshape(16, 128, 2048))
    pm = np.asarray(wd["L0_attn_pm"])
    pmk = np.ascontiguousarray(pm[0].transpose(0, 2, 1))
    pmv = np.ascontiguousarray(pm[1])
    wg1T = np.ascontiguousarray(np.asarray(wd["L0_ffn_Wg1"]).T.reshape(16, 128, 8))
    wg2T = np.ascontiguousarray(np.asarray(wd["L0_ffn_Wg2"]).T)
    wup = np.ascontiguousarray(
        np.asarray(wd["L0_ffn_Wup"]).transpose(0, 2, 1)
        .reshape(8, 16, 128, 8, 128).transpose(0, 3, 2, 1, 4).reshape(8, 8, 128, 2048))
    wdT = np.ascontiguousarray(
        np.asarray(wd["L0_ffn_Wdown"]).transpose(0, 2, 1).reshape(8, 8, 128, 2048))
    bup = np.ascontiguousarray(np.asarray(wd["L0_ffn_bup"]).reshape(8, 8, 128, 1))
    bdown = np.ascontiguousarray(np.asarray(wd["L0_ffn_bdown"]))
    normw = np.ascontiguousarray(np.asarray(wd["L0_attn_norm_w"])[None, :])
    m = {
        "xT": xt, "posB": posB, "k2col": k2col, "altcol": altcol,
        "identb": np.asarray(ident), "identf": identf,
        "ones2": np.asarray(ones2),
        "sel8": np.asarray(sel8), "maskqt": maskqt, "mask01T": mask01T,
        "wqk": qk, "wv": wv, "wo": wo, "pmk": pmk, "pmv": pmv,
        "wg1T": wg1T, "wg2T": wg2T, "wupT": wup, "wdT": wdT,
        "bup": bup, "bdown": bdown, "normw": normw,
    }
    if shared is not None:
        for k, v in m.items():
            if k not in ("xT", "posB"):
                shared[k] = v
    return m


def declare_inputs(nc):
    d = {}

    def di(name, shape, dt=F32):
        d[name] = nc.dram_tensor(name, list(shape), dt, kind="ExternalInput").ap()
    di("xT", (D, NT))
    di("posB", (128, NT))
    di("k2col", (128, 1))
    di("altcol", (128, 1))
    di("identb", (128, 128), BF16)
    di("identf", (8, 8))
    di("ones2", (2, 128), BF16)
    di("sel8", (8, 1024), BF16)
    di("maskqt", (4, 128, KT))
    di("mask01T", (4, 128, W), BF16)
    di("wqk", (32, 128, 2048))
    di("wv", (16, 128, 2048))
    di("wo", (16, 128, 2048))
    di("pmk", (16, 128, 16))
    di("pmv", (16, 16, 128))
    di("wg1T", (16, 128, 8))
    di("wg2T", (8, 8))
    di("wupT", (8, 8, 128, 2048))
    di("wdT", (8, 8, 128, 2048))
    di("bup", (8, 8, 128, 1))
    di("bdown", (8, 2048))
    di("normw", (1, 2048))
    return d


class MK:
    def __init__(self, stage=6, debug=True):
        self.stage = stage
        self.debug = debug
        self.nc = bass.Bass("TRN2", target_bir_lowering=False, debug=False)
        self.I = declare_inputs(self.nc)
        self.b = Builder(self.nc)
        self.outs = {}
        self.stk = contextlib.ExitStack()

    def out(self, name, shape, dt=F32):
        ap = self.nc.dram_tensor(name, list(shape), dt, kind="ExternalOutput").ap()
        self.outs[name] = ap
        return ap

    def sb(self, name, shape, dt=F32, stk=None):
        return Tile((stk or self.stk).enter_context(
            self.nc.sbuf_tensor(name, list(shape), dt)), name)

    def psb(self, name, shape=(128, 512)):
        return Tile(self.stk.enter_context(
            self.nc.psum_tensor(name, list(shape), F32)), name)

    _dumpn = 0

    def _dump(self, name, tile):
        o = self.out(name, list(tile.t.shape), tile.t.dtype)
        MK._dumpn += 1
        self.b.op("sync", (lambda o=o, t=tile: lambda e: e.dma_start(
            out=o, in_=t.ap))(), reads=[tile], dma=f"d_dbg{MK._dumpn}")

    def _dump_into(self, dst_ap, tile, col=None):
        src = tile.ap if col is None else tile[:, col]
        self.b.dma_out(dst_ap, src, reads=[tile])

    def build(self):
        b, I = self.b, self.I
        self.banks = [self.psb(f"pb{i}") for i in range(8)]

        self.af = self.sb("af_arena", [128, 7760], F32)
        self.ab = self.sb("ab_arena", [128, 53376], BF16)

        def afv(name, a, cols, p=128):
            return Tile(self.af.t[0:p, a:a + cols] if p != 128
                        else self.af.t[:, a:a + cols], name)

        def abv(name, a, cols, p=128):
            return Tile(self.ab.t[0:p, a:a + cols] if p != 128
                        else self.ab.t[:, a:a + cols], name)

        self.x = [self.sb(f"x{i}", [128, NT]) for i in range(NC_F)]
        self.identb = self.sb("identb_s", [128, 128], BF16)
        self.ones2 = self.sb("ones2_s", [2, 128], BF16)
        self.onescol = self.sb("onescol", [128, 2], BF16)
        self.rowb = self.sb("rowbscr", [1, 1024], BF16)
        self.si = self.sb("tscri", [128, 256], I32)
        t_k2 = self.sb("k2_s", [128, 1])
        t_alt = self.sb("alt_s", [128, 1])
        invf = self.sb("invf", [128, 1])
        negmax8 = self.sb("negmax", [128, 8])
        negD8 = self.sb("negD", [128, 8])
        rsum8 = self.sb("rsum", [128, 8])
        locol8 = self.sb("locol", [128, 8])
        hi328 = self.sb("hi32", [128, 8])
        hilo16 = self.sb("hilo", [128, 16], BF16)
        def _sp(t, name):
            return [Tile(t.t[:, 0:t.t.shape[1] // 2], name + "0"),
                    Tile(t.t[:, t.t.shape[1] // 2:], name + "1")]
        self.negmax = _sp(negmax8, "negmax")
        self.negD = _sp(negD8, "negD")
        self.rsum = _sp(rsum8, "rsum")
        self.locol = _sp(locol8, "locol")
        self.hi32 = _sp(hi328, "hi32")
        self.hilo = _sp(hilo16, "hilo")
        self.pmk = [self.sb(f"pmk{h}", [128, NP], BF16) for h in range(NH)]
        self.pmv = [self.sb(f"pmv{h}", [NP, 128], BF16) for h in range(NH)]
        self.gl = self.sb("gl", [128, 8])
        self.gscr = self.sb("gscr", [128, 8])
        self.gm = self.sb("gm", [128, 4])
        self.identf = self.sb("identf_s", [8, 8])
        self.wg2 = self.sb("wg2", [8, 8], BF16)
        self.wg2.dsem = "d_wg2"
        self.bupcol = [self.sb(f"bupc{i}", [128, 1]) for i in range(2)]
        for i, t in enumerate(self.bupcol):
            t.dsem = f"d_bup{i}"
        pmscr = self.sb("pmscr", [128, NP])
        pmscr.dsem = "d_pmscr"

        self.cos_s = afv("cos_s", 0, 1024)
        self.sin_a = afv("sin_a", 1024, 1024)
        t_posB = afv("posB_s", 2048, 1024)
        th = afv("theta", 3072, 1024)
        ts1 = afv("tscr1", 4096, 1024)
        ts2 = afv("tscr2", 5120, 1024)
        self.maskqt = [afv(f"maskqt{i}", 2048 + 528 * i, 528) for i in range(4)]
        self.sim = afv("sim", 4160, 528)
        self.sim1 = afv("sim1", 4688, 528)
        self.rscr = [afv("rscr0", 4688, 512), afv("rscr1", 5200, 512)]
        self.sim1ov = self.sim1
        self.lnscr = afv("lnscr", 5712, 512)
        self.row = afv("rowscr", 6224, 1536, p=1)

        pool64 = [abv(f"bb{i}", 512 * i, 512) for i in range(64)]
        self.qh = pool64[0:16]
        self.kh = pool64[16:32]
        self.xn = pool64[32:48]
        self.attnout = self.xn
        self.vtm = [abv(f"v{tc}", 24576 + 2048 * tc, 2048) for tc in range(4)]
        self.wbuf = [abv(f"wbuf{i}", 32768 + 2048 * i, 2048) for i in range(4)]
        for i, t in enumerate(self.wbuf):
            t.dsem = f"d_wb{i}"
        AE = 40960
        self.mask01 = [abv(f"mask01_{i}", AE + 512 * i, 512) for i in range(4)]
        self.normwb = abv("normwb", AE + 2048, 2048, p=1)
        self.att = [abv(f"att{i}", AE + 4096 + 512 * i, 512) for i in range(4)]
        self.eq = abv("eq", AE + 7168, 528)
        self.attp = abv("attp", AE + 6144, 512, p=NP)
        self.drow = abv("drow", AE + 6656, 512, p=2)
        self.xth = [abv(f"xt{i}", AE + 512 * i, 512) for i in range(NC_F)]
        GB = AE + 8192
        self.sbb = [abv("sbb0", GB, 512), abv("sbb1", GB + 512, 512)]
        self.selb = abv("selb", GB + 1024, 1024, p=8)
        self.selb.dsem = "d_selb"
        self.score_em = abv("score_em", GB + 2048, 1024, p=8)
        self.wg1 = abv("wg1", GB + 3072, 128)
        self.wg1.dsem = "d_wg1"
        self.bdn = abv("bdn", GB + 3200, 1024, p=8)
        self.bdn.dsem = "d_bdn"
        assert GB + 4224 <= 53376, GB + 4224

        for i in range(NC_F):
            self.x[i].dsem = f"d_x{i}"
            b.dma("sync", self.x[i].ap, I["xT"][i * 128:(i + 1) * 128, :],
                  writes=[self.x[i]])
        b.dma("sync", self.identb.ap, I["identb"][:], writes=[self.identb])
        b.dma("sync", self.identf.ap, I["identf"][:], writes=[self.identf])
        b.dma("sync", self.ones2.ap, I["ones2"][:], writes=[self.ones2])
        b.op("dve", lambda v: v.memset(self.onescol.ap, 1.0),
             writes=[self.onescol])
        b.dma("sync", t_posB.ap, I["posB"][:], writes=[t_posB])
        b.dma("sync", t_k2.ap, I["k2col"][:], writes=[t_k2])
        b.dma("sync", t_alt.ap, I["altcol"][:], writes=[t_alt])
        self._trig_tables(t_posB, t_k2, t_alt, invf, th, ts1, ts2)
        b.barrier()
        b.set_init_phase(2)
        b.dma("pool", self.normwb.ap, I["normw"][:], writes=[self.normwb])
        for i in range(4):
            b.dma("sync", self.maskqt[i].ap, I["maskqt"][i],
                  writes=[self.maskqt[i]])
            b.dma("sync", self.mask01[i].ap, I["mask01T"][i],
                  writes=[self.mask01[i]])
        s4c = float(DH) ** -0.25
        for h in range(NH):
            b.dma("sync", pmscr.ap, I["pmk"][h], writes=[pmscr])
            b.op("dve", (lambda h=h: lambda v: v.tensor_single_scalar(
                out=self.pmk[h].ap, in_=pmscr.ap, scalar=s4c, op=ALU.mult))(),
                reads=[pmscr], writes=[self.pmk[h]])
            b.dma("pool", self.pmv[h].ap, I["pmv"][h], writes=[self.pmv[h]])

        if self.debug and self.stage == 0:
            self._dump("dbg_cos", self.cos_s)
            self._dump("dbg_sin", self.sin_a)

        for w in (0, 1):
            self._layernorm(w, fuse_rms=True, xsrc=self.x, dst=self.xn)
            if self.debug and self.stage == 1 and w == 0:
                o = self.out("dbg_xn", [NC_F, 128, W], BF16)
                for i in range(NC_F):
                    self._dump_into(o[i], self.xn[i])
                break
            self._qkv_rope(w)
            if self.debug and self.stage == 2 and w == 0:
                oq = self.out("dbg_q", [NH, 128, W], BF16)
                ok = self.out("dbg_k", [NH, 128, W], BF16)
                ov = self.out("dbg_v", [4, 128, D], BF16)
                for i in range(NH):
                    self._dump_into(oq[i], self.qh[i])
                    self._dump_into(ok[i], self.kh[i])
                for i in range(4):
                    self._dump_into(ov[i], self.vtm[i])
                break
            self._outproj_prefetch(w)
            for h in range(NH + 1):
                if h < NH:
                    self._attn_qt(w, h)
                if h >= 1:
                    self._attn_kt(w, h - 1)
            if self.debug and self.stage == 3 and w == 0:
                o = self.out("dbg_ao", [NC_F, 128, W], BF16)
                for i in range(NC_F):
                    self._dump_into(o[i], self.attnout[i])
                break
            self._outproj(w)

        if self.stage <= 3:
            self.b.finalize()
            return self
        if self.debug and self.stage == 4:
            o = self.out("dbg_x1", [NC_F, 128, NT])
            for i in range(NC_F):
                self._dump_into(o[i], self.x[i])
            self.b.finalize()
            return self

        for t in (self.qh + self.kh + self.xn + self.vtm + self.mask01 +
                  self.att + [self.eq, self.attp, self.drow, self.normwb,
                              self.sim] + self.maskqt):
            t.dead = True
        b.barrier()
        b.set_init_phase(3)
        self._ffn_phase(pool64)
        if self.stage >= 6:
            oy = self.out("y", [D, NT])
            for i in range(NC_F):
                self._dump_into(oy[i * 128:(i + 1) * 128, :], self.x[i])
        self.b.finalize()
        return self

    def _trig_tables(self, t_posB, t_k2, t_alt, invf, th, s1, s2):
        b = self.b
        si = self.si
        b.op("act", lambda s: s.activation(
            out=invf.ap, in_=t_k2.ap, func=AF.Exp,
            scale=-math.log(10000.0) / 128.0), reads=[t_k2], writes=[invf])
        b.op("dve", lambda v: v.tensor_scalar(
            out=th.ap, in0=t_posB.ap, scalar1=invf[:, 0:1], scalar2=None,
            op0=ALU.mult), reads=[t_posB, invf], writes=[th])

        def trig(dst, shift, mult_ap_tile, mult_imm):
            inv2pi = 1.0 / (2.0 * math.pi)
            b.op("dve", lambda v: v.tensor_scalar(
                out=s1.ap, in0=th.ap, scalar1=inv2pi, scalar2=shift * inv2pi,
                op0=ALU.mult, op1=ALU.add), reads=[th], writes=[s1])
            for hf in range(4):
                c = slice(hf * 256, (hf + 1) * 256)
                b.op("dve", (lambda c=c: lambda v: v.tensor_copy(
                    si.ap, s1[:, c]))(), reads=[s1], writes=[si])
                b.op("dve", (lambda c=c: lambda v: v.tensor_copy(
                    s1[:, c], si.ap))(), reads=[si], writes=[s1])
            b.op("dve", lambda v: v.scalar_tensor_tensor(
                out=s2.ap, in0=s1.ap, scalar=-2.0 * math.pi, in1=th.ap,
                op0=ALU.mult, op1=ALU.add), reads=[s1, th], writes=[s2])
            if shift != 0.0:
                b.op("dve", lambda v: v.tensor_single_scalar(
                    out=s2.ap, in_=s2.ap, scalar=shift, op=ALU.add),
                    reads=[s2], writes=[s2])
            b.op("dve", lambda v: v.tensor_single_scalar(
                out=s1.ap, in_=s2.ap, scalar=math.pi, op=ALU.is_gt),
                reads=[s2], writes=[s1])
            b.op("dve", lambda v: v.scalar_tensor_tensor(
                out=s2.ap, in0=s1.ap, scalar=-2.0 * math.pi, in1=s2.ap,
                op0=ALU.mult, op1=ALU.add), reads=[s1, s2], writes=[s2])
            b.op("dve", lambda v: v.tensor_single_scalar(
                out=s1.ap, in_=s2.ap, scalar=-math.pi, op=ALU.is_lt),
                reads=[s2], writes=[s1])
            b.op("dve", lambda v: v.scalar_tensor_tensor(
                out=s2.ap, in0=s1.ap, scalar=2.0 * math.pi, in1=s2.ap,
                op0=ALU.mult, op1=ALU.add), reads=[s1, s2], writes=[s2])
            b.op("dve", lambda v: v.tensor_scalar(
                out=s2.ap, in0=s2.ap, scalar1=math.pi, scalar2=-math.pi,
                op0=ALU.min, op1=ALU.max), reads=[s2], writes=[s2])
            b.op("act", lambda s: s.activation(
                out=s1.ap, in_=s2.ap, func=AF.Sin), reads=[s2], writes=[s1])
            if mult_ap_tile is not None:
                b.op("dve", lambda v: v.tensor_scalar(
                    out=dst.ap, in0=s1.ap, scalar1=mult_ap_tile[:, 0:1],
                    scalar2=None, op0=ALU.mult),
                    reads=[s1, mult_ap_tile], writes=[dst])
            else:
                b.op("dve", lambda v: v.tensor_single_scalar(
                    out=dst.ap, in_=s1.ap, scalar=mult_imm, op=ALU.mult),
                    reads=[s1], writes=[dst])

        s4 = float(DH) ** -0.25
        trig(self.sin_a, 0.0, t_alt, None)
        trig(self.cos_s, math.pi / 2.0, None, s4)

    def _layernorm(self, w, fuse_rms, xsrc, dst, tgt_dt=BF16):
        b = self.b
        sl = slice(w * W, (w + 1) * W)
        S0, S1, PA, PB, PA2, PB2 = self.banks[0:6]
        oc = self.onescol[:, 0:1]
        for i in range(NC_F):
            cb, sq = self.sbb[0], self.sbb[1]
            b.op("dve", (lambda i=i, cb=cb: lambda v: v.tensor_copy(
                cb.ap, xsrc[i][:, sl]))(), reads=[xsrc[i]], writes=[cb])
            b.op("act", (lambda i=i, sq=sq: lambda s: s.activation(
                out=sq.ap, in_=xsrc[i][:, sl], func=AF.Square))(),
                reads=[xsrc[i]], writes=[sq])
            b.mm(S0, S0.t[0:1, :], oc, cb.ap,
                 reads=[self.onescol, cb], start=(i == 0),
                 stop=(i == NC_F - 1), token=True)
            b.mm(S1, S1.t[0:1, :], oc, sq.ap,
                 reads=[self.onescol, sq], start=(i == 0),
                 stop=(i == NC_F - 1), token=True)
        r, rb = self.row, self.rowb
        mrow, erow = r[0:1, 0:512], r[0:1, 512:1024]
        msq = r[0:1, 1024:1536]
        varr, denr = r[0:1, 512:1024], r[0:1, 512:1024]
        alph, malph = r[0:1, 1024:1536], r[0:1, 512:1024]
        b.op("dve", lambda v: v.tensor_scalar(
            out=mrow, in0=S0.t[0:1, :], scalar1=1.0 / D, scalar2=None,
            op0=ALU.mult), reads=[S0], writes=[r])
        b.op("dve", lambda v: v.tensor_scalar(
            out=erow, in0=S1.t[0:1, :], scalar1=1.0 / D, scalar2=None,
            op0=ALU.mult), reads=[S1, r], writes=[r])
        b.op("dve", lambda v: v.tensor_tensor(
            out=msq, in0=mrow, in1=mrow, op=ALU.mult), reads=[r], writes=[r])
        b.op("dve", lambda v: v.tensor_tensor(
            out=varr, in0=erow, in1=msq, op=ALU.subtract),
            reads=[r], writes=[r])
        if fuse_rms:
            b.op("dve", lambda v: v.tensor_scalar(
                out=denr, in0=varr, scalar1=1.0 + 1e-6, scalar2=1e-11,
                op0=ALU.mult, op1=ALU.add), reads=[r], writes=[r])
        else:
            b.op("dve", lambda v: v.tensor_single_scalar(
                out=denr, in_=varr, scalar=1e-5, op=ALU.add),
                reads=[r], writes=[r])
        b.op("act", lambda s: s.activation(out=denr, in_=denr, func=AF.Ln),
             reads=[r], writes=[r])
        b.op("act", lambda s: s.activation(out=alph, in_=denr, func=AF.Exp,
             scale=-0.5), reads=[r], writes=[r])
        b.op("dve", lambda v: v.tensor_tensor(
            out=malph, in0=mrow, in1=alph, op=ALU.mult), reads=[r], writes=[r])
        arow, marow = rb[0:1, 0:512], rb[0:1, 512:1024]
        b.op("dve", lambda v: v.tensor_copy(arow, alph), reads=[r], writes=[rb])
        b.op("dve", lambda v: v.tensor_copy(marow, malph), reads=[r],
             writes=[rb])
        for i in range(NC_F):
            BA = PA if i % 2 == 0 else PA2
            BB = PB if i % 2 == 0 else PB2
            if fuse_rms:
                lh, lht = self.normwb[0:1, i * 128:(i + 1) * 128], self.normwb
            else:
                lh, lht = self.ones2[0:1, :], self.ones2
            b.mm(BA, BA.ap, lh, arow, reads=[lht, self.rowb],
                 start=True, stop=True)
            b.mm(BB, BB.ap, lh, marow, reads=[lht, self.rowb],
                 start=True, stop=True)
            b.op("dve", (lambda i=i, BA=BA: lambda v: v.tensor_tensor(
                out=self.lnscr.ap, in0=xsrc[i][:, sl], in1=BA.ap,
                op=ALU.mult))(), reads=[xsrc[i], BA], writes=[self.lnscr])
            b.op("dve", (lambda i=i, BB=BB: lambda v: v.tensor_tensor(
                out=dst[i].ap, in0=self.lnscr.ap, in1=BB.ap,
                op=ALU.subtract))(), reads=[self.lnscr, BB], writes=[dst[i]])

    def _qkv_rope(self, w):
        b, I = self.b, self.I
        sl = slice(w * W, (w + 1) * W)
        BQ = [self.banks[6], self.banks[7]]
        WB = 32768
        for jp in range(16):
            pa = (jp % 2) * 2
            pair_ap = self.ab.t[:, WB + pa * 2048:WB + (pa + 2) * 2048]
            b.dma("pool", pair_ap.rearrange("p (n j) -> p n j", n=2),
                  I["wqk"][2 * jp:2 * jp + 2].rearrange("n p j -> p n j"),
                  writes=[self.wbuf[pa], self.wbuf[pa + 1]])
            for jl in range(2):
                jc = 2 * jp + jl
                buf = self.wbuf[pa + jl]
                bank = BQ[jc % 2]
                for kc in range(NC_F):
                    b.mm(bank, bank.ap, buf[:, kc * 128:(kc + 1) * 128],
                         self.xn[kc].ap,
                         reads=[buf, self.xn[kc]] if kc == 0
                         else [self.xn[kc]],
                         start=(kc == 0), stop=(kc == NC_F - 1))
                dst = self.qh[jc] if jc < 16 else self.kh[jc - 16]
                self._rope(bank, dst, sl)
        BV = self.banks[0:4]
        WB = 32768
        for jb in range(4):
            for dh8 in range(2):
                pa = ((jb * 2 + dh8) % 2) * 2
                pair_ap = self.ab.t[:, WB + pa * 2048:WB + (pa + 2) * 2048]
                wts = [self.wbuf[pa], self.wbuf[pa + 1]]
                b.dma("pool", pair_ap.rearrange("p (g j) -> p g j", g=8),
                      I["wv"][dh8 * 8:(dh8 + 1) * 8, :,
                              jb * 512:(jb + 1) * 512]
                      .rearrange("g p j -> p g j"), writes=wts)
                for dcl in range(8):
                    dc = dh8 * 8 + dcl
                    vb = wts[dcl // 4]
                    for tc in range(4):
                        b.mm(BV[tc], BV[tc].ap,
                             self.xn[dc][:, tc * 128:(tc + 1) * 128],
                             vb[:, (dcl % 4) * 512:(dcl % 4 + 1) * 512],
                             reads=[self.xn[dc], vb],
                             start=(dc == 0), stop=(dc == 15),
                             token=(dcl % 4 == 3 and tc == 3))
            for tc in range(4):
                b.op("dve", (lambda tc=tc, jb=jb: lambda v: v.tensor_copy(
                    self.vtm[tc][:, jb * 512:(jb + 1) * 512], BV[tc].ap))(),
                    reads=[BV[tc]], writes=[self.vtm[tc]])

    def _rope(self, bank, dst, sl):
        b = self.b
        r0, r1 = self.rscr[0], self.lnscr
        swapmask = [(i // 2) * 2 + (1 - (i % 2)) for i in range(32)]
        b.op("dve", lambda v: v.stream_shuffle(r0.ap, bank.ap, mask=swapmask),
             reads=[bank], writes=[r0, self.sim1ov])
        b.op("dve", (lambda sl=sl: lambda v: v.tensor_tensor(
            out=r1.ap, in0=bank.ap, in1=self.cos_s[:, sl], op=ALU.mult))(),
            reads=[bank, self.cos_s], writes=[r1])
        b.op("dve", (lambda sl=sl: lambda v: v.tensor_tensor(
            out=r0.ap, in0=r0.ap, in1=self.sin_a[:, sl], op=ALU.mult))(),
            reads=[r0, self.sin_a], writes=[r0, self.sim1ov])
        b.op("dve", (lambda dst=dst: lambda v: v.tensor_tensor(
            out=dst.ap, in0=r1.ap, in1=r0.ap, op=ALU.add))(),
            reads=[r0, r1], writes=[dst])

    def _attn_qt(self, w, h):
        b = self.b
        par = h % 2
        q, k = self.qh[h], self.kh[h]
        SA = self.banks[par]
        SB = self.banks[2 + par]
        sim = self.sim if par == 0 else self.sim1
        NM, RS, ND = self.negmax[par], self.rsum[par], self.negD[par]
        LC, H32, HL = self.locol[par], self.hi32[par], self.hilo[par]
        for qc in range(4):
            nmx = NM[:, qc:qc + 1]
            rs = RS[:, qc:qc + 1]
            qsl = q[:, qc * 128:(qc + 1) * 128]
            wn = 128 * (qc + 1)
            if qc < 3:
                tot = NP + wn
                b.mm(SA, SA.t[:, 0:NP], qsl, self.pmk[h].ap,
                     reads=[q, self.pmk[h]], start=True, stop=False)
                b.mm(SA, SA.t[:, NP:tot], qsl, k[:, 0:wn],
                     reads=[k], start=False, stop=True)
                b.op("dve", (lambda qc=qc, sim=sim, SA=SA, tot=tot:
                     lambda v: v.tensor_tensor(
                         out=sim[:, 0:tot], in0=SA.t[:, 0:tot],
                         in1=self.maskqt[qc][:, 0:tot], op=ALU.add))(),
                    reads=[SA, self.maskqt[qc]], writes=[sim])
            else:
                tot = KT
                b.mm(SA, SA.ap, qsl, k.ap, reads=[q, k],
                     start=True, stop=True)
                b.mm(SB, SB.t[:, 0:NP], qsl, self.pmk[h].ap,
                     reads=[q, self.pmk[h]], start=True, stop=True)
                b.op("dve", (lambda sim=sim, SB=SB: lambda v: v.tensor_copy(
                    sim[:, 0:NP], SB.t[:, 0:NP]))(),
                    reads=[SB], writes=[sim])
                b.op("dve", (lambda qc=qc, sim=sim, SA=SA:
                     lambda v: v.tensor_tensor(
                         out=sim[:, NP:], in0=SA.ap,
                         in1=self.maskqt[qc][:, NP:], op=ALU.add))(),
                    reads=[SA, self.maskqt[qc]], writes=[sim])
            b.op("dve", (lambda sim=sim, nmx=nmx, tot=tot:
                 lambda v: v.reduce_max(
                     out=nmx, in_=sim[:, 0:tot], axis=AX.X, negate=True))(),
                reads=[sim], writes=[NM])
            b.op("act", (lambda sim=sim, nmx=nmx, rs=rs, tot=tot:
                 lambda s: s.activation(
                     out=self.eq[:, 0:tot], in_=sim[:, 0:tot], func=AF.Exp,
                     bias=nmx, accum_out=rs))(), reads=[sim, NM],
                writes=[self.eq, RS])
        b.op("act", (lambda RS=RS: lambda s: s.activation(
            out=RS.ap, in_=RS.ap, func=AF.Ln))(),
            reads=[RS], writes=[RS])
        b.op("dve", (lambda NM=NM, RS=RS, ND=ND: lambda v: v.tensor_tensor(
            out=ND.ap, in0=NM.ap, in1=RS.ap, op=ALU.subtract))(),
            reads=[NM, RS], writes=[ND])
        hie = HL.t.rearrange("p (a b) -> p a b", b=2)[:, :, 0]
        hio = HL.t.rearrange("p (a b) -> p a b", b=2)[:, :, 1]
        b.op("dve", (lambda hie=hie, ND=ND, HL=HL: lambda v: v.tensor_copy(
            hie, ND.ap))(), reads=[ND], writes=[HL])
        b.op("dve", (lambda hie=hie, H32=H32, HL=HL: lambda v: v.tensor_copy(
            H32.ap, hie))(), reads=[HL], writes=[H32])
        b.op("dve", (lambda ND=ND, H32=H32, LC=LC: lambda v: v.tensor_tensor(
            out=LC.ap, in0=ND.ap, in1=H32.ap, op=ALU.subtract))(),
            reads=[ND, H32], writes=[LC])
        b.op("dve", (lambda hio=hio, LC=LC, HL=HL: lambda v: v.tensor_copy(
            hio, LC.ap))(), reads=[LC], writes=[HL])

    def _attn_kt(self, w, h):
        b = self.b
        par = h % 2
        q, k = self.qh[h], self.kh[h]
        TD = self.banks[4]
        SK = [self.banks[5], self.banks[6]]
        AV = self.banks[7]
        HL = self.hilo[par]
        for qc in range(4):
            b.mm(TD, TD.t[0:2, qc * 128:(qc + 1) * 128],
                 HL[:, 2 * qc:2 * qc + 2],
                 self.identb.ap,
                 reads=[HL, self.identb],
                 start=(qc == 0), stop=(qc == 3))
        b.op("dve", lambda v: v.tensor_copy(self.drow.ap, TD.t[0:2, :]),
             reads=[TD], writes=[self.drow])
        for c in range(5):
            bank = SK[c % 2]
            qt0 = 0 if c == 0 else 128 * (c - 1)
            n = W - qt0
            if c == 0:
                lh_k, lht, cols, dst = self.pmk[h].ap, self.pmk[h], NP, self.attp
            else:
                lh_k = k[:, (c - 1) * 128:c * 128]
                lht, cols, dst = k, 128, self.att[c - 1]
            dstp = bank.t[0:cols, 0:n]
            b.mm(bank, dstp, self.ones2[0:2, 0:cols], self.drow[0:2, qt0:W],
                 reads=[self.ones2, self.drow], start=True, stop=False)
            b.mm(bank, dstp, lh_k, q[:, qt0:W], reads=[lht, q],
                 start=False, stop=True)
            b.op("act", (lambda dstp=dstp, dst=dst, qt0=qt0:
                 lambda s: s.activation(
                     out=dst[:, qt0:W] if qt0 else dst.ap, in_=dstp,
                     func=AF.Exp))(),
                reads=[bank], writes=[dst])
            if c >= 1:
                b.op("dve", (lambda c=c, dst=dst, qt0=qt0:
                     lambda v: v.tensor_tensor(
                         out=dst[:, qt0:W], in0=dst[:, qt0:W],
                         in1=self.mask01[c - 1][:, qt0:W],
                         op=ALU.mult))(), reads=[dst, self.mask01[c - 1]],
                    writes=[dst])
        b.mm(AV, AV.ap, self.pmv[h].ap, self.attp.ap,
             reads=[self.pmv[h], self.attp], start=True, stop=False)
        for c in range(1, 5):
            qt0 = 128 * (c - 1)
            b.mm(AV, AV.t[:, qt0:W],
                 self.vtm[c - 1][:, h * 128:(h + 1) * 128],
                 self.att[c - 1][:, qt0:W],
                 reads=[self.vtm[c - 1], self.att[c - 1]],
                 start=False, stop=(c == 4))
        b.op("dve", (lambda h=h: lambda v: v.tensor_copy(
            self.attnout[h].ap, AV.ap))(), reads=[AV],
            writes=[self.attnout[h]])
        if self.debug and self.stage == 31 and w == 0 and h == 0:
            self._dump("dbg_negmax", self.negmax)
            self._dump("dbg_rsum", self.rsum)
            self._dump("dbg_negD", self.negD)
            self._dump("dbg_hilo", self.hilo)
            self._dump("dbg_drow", self.drow)

    def _outproj_prefetch(self, w):
        b, I = self.b, self.I
        WB = 32768
        for oc in (0, 2):
            pa = (oc // 2 % 2) * 2
            pair_ap = self.ab.t[:, WB + pa * 2048:WB + (pa + 2) * 2048]
            b.dma("pool", pair_ap.rearrange("p (n j) -> p n j", n=2),
                  I["wo"][oc:oc + 2].rearrange("n p j -> p n j"),
                  writes=[self.wbuf[pa], self.wbuf[pa + 1]])

    def _outproj(self, w):
        b, I = self.b, self.I
        sl = slice(w * W, (w + 1) * W)
        BO = [self.banks[0], self.banks[1]]
        WB = 32768
        for oc in range(NC_F):
            buf = self.wbuf[(oc // 2 % 2) * 2 + oc % 2]
            if oc % 2 == 0 and oc >= 4:
                pa = (oc // 2 % 2) * 2
                pair_ap = self.ab.t[:, WB + pa * 2048:WB + (pa + 2) * 2048]
                b.dma("pool", pair_ap.rearrange("p (n j) -> p n j", n=2),
                      I["wo"][oc:oc + 2].rearrange("n p j -> p n j"),
                      writes=[self.wbuf[pa], self.wbuf[pa + 1]])
            bank = BO[oc % 2]
            for kc in range(NC_F):
                b.mm(bank, bank.ap, buf[:, kc * 128:(kc + 1) * 128],
                     self.attnout[kc].ap,
                     reads=[buf, self.attnout[kc]] if kc == 0
                     else [self.attnout[kc]],
                     start=(kc == 0), stop=(kc == NC_F - 1))
            b.op("dve", (lambda oc=oc, bank=bank: lambda v: v.tensor_tensor(
                out=self.x[oc][:, sl], in0=self.x[oc][:, sl], in1=bank.ap,
                op=ALU.add))(), reads=[bank, self.x[oc]], writes=[self.x[oc]])

    def _ffn_phase(self, pool64):
        b, I = self.b, self.I
        hp = [[Tile(self.ab.t[:, 512 * (8 * e + hc):512 * (8 * e + hc) + 512],
                    f"hp{e}_{hc}") for hc in range(8)] for e in range(NE)]
        self.lgf = Tile(self.af.t[0:8, 2048:2560], "lgf")
        self.g64 = Tile(self.af.t[:, 2560:2592], "g64")
        self.gA = Tile(self.af.t[:, 2592:2624], "gA")
        self.gC = Tile(self.af.t[:, 2624:2656], "gC")
        self.gm4 = Tile(self.af.t[:, 2656:2676], "gm4")
        b.dma("pool", self.wg1.ap.rearrange("p (c e) -> p c e", c=16),
              I["wg1T"].rearrange("c p e -> p c e"), writes=[self.wg1])
        b.dma("pool", self.wg2.ap, I["wg2T"][:], writes=[self.wg2])
        b.dma("sync", self.selb.ap, I["sel8"][:], writes=[self.selb])
        for half in range(2):
            self._layernorm_cols(half, xsrc=self.x, dst_tiles=self.xth)
            WB = 32768
            pair_ap = self.ab.t[:, WB:WB + 4096]
            b.dma("pool", pair_ap.rearrange("p (n j) -> p n j", n=2),
                  I["wupT"][0, 0:2].rearrange("n p j -> p n j"),
                  writes=[self.wbuf[0], self.wbuf[1]])
            self._gate(half)
            if self.debug and self.stage == 45 and half == 0:
                o = self.out("dbg_xt", [NC_F, 128, W], BF16)
                for i in range(NC_F):
                    self._dump_into(o[i], self.xth[i])
                return
            if self.debug and self.stage in (5, 51) and half == (1 if self.stage == 5 else 0):
                self._dump("dbg_scores", self.score_em)
                return
            self._ffn_half(half, hp)

    def _layernorm_cols(self, w, xsrc, dst_tiles):
        b = self.b
        sl = slice(w * W, (w + 1) * W)
        S0, S1, PA, PB, PA2, PB2 = self.banks[0:6]
        oc = self.onescol[:, 0:1]
        for i in range(NC_F):
            cb, sq = self.sbb[0], self.sbb[1]
            b.op("dve", (lambda i=i, cb=cb: lambda v: v.tensor_copy(
                cb.ap, xsrc[i][:, sl]))(), reads=[xsrc[i]], writes=[cb])
            b.op("act", (lambda i=i, sq=sq: lambda s: s.activation(
                out=sq.ap, in_=xsrc[i][:, sl], func=AF.Square))(),
                reads=[xsrc[i]], writes=[sq])
            b.mm(S0, S0.t[0:1, :], oc, cb.ap, reads=[self.onescol, cb],
                 start=(i == 0), stop=(i == NC_F - 1), token=True)
            b.mm(S1, S1.t[0:1, :], oc, sq.ap, reads=[self.onescol, sq],
                 start=(i == 0), stop=(i == NC_F - 1), token=True)
        r, rb = self.row, self.rowb
        mrow, erow = r[0:1, 0:512], r[0:1, 512:1024]
        msq = r[0:1, 1024:1536]
        varr, denr = r[0:1, 512:1024], r[0:1, 512:1024]
        alph, malph = r[0:1, 1024:1536], r[0:1, 512:1024]
        b.op("dve", lambda v: v.tensor_scalar(
            out=mrow, in0=S0.t[0:1, :], scalar1=1.0 / D, scalar2=None,
            op0=ALU.mult), reads=[S0], writes=[r])
        b.op("dve", lambda v: v.tensor_scalar(
            out=erow, in0=S1.t[0:1, :], scalar1=1.0 / D, scalar2=None,
            op0=ALU.mult), reads=[S1, r], writes=[r])
        b.op("dve", lambda v: v.tensor_tensor(
            out=msq, in0=mrow, in1=mrow, op=ALU.mult), reads=[r], writes=[r])
        b.op("dve", lambda v: v.tensor_tensor(
            out=varr, in0=erow, in1=msq, op=ALU.subtract),
            reads=[r], writes=[r])
        b.op("dve", lambda v: v.tensor_single_scalar(
            out=denr, in_=varr, scalar=1e-5, op=ALU.add), reads=[r], writes=[r])
        b.op("act", lambda s: s.activation(out=denr, in_=denr, func=AF.Ln),
             reads=[r], writes=[r])
        b.op("act", lambda s: s.activation(out=alph, in_=denr, func=AF.Exp,
             scale=-0.5), reads=[r], writes=[r])
        b.op("dve", lambda v: v.tensor_tensor(
            out=malph, in0=mrow, in1=alph, op=ALU.mult), reads=[r], writes=[r])
        arow, marow = rb[0:1, 0:512], rb[0:1, 512:1024]
        b.op("dve", lambda v: v.tensor_copy(arow, alph), reads=[r], writes=[rb])
        b.op("dve", lambda v: v.tensor_copy(marow, malph), reads=[r],
             writes=[rb])
        for i in range(NC_F):
            BA = PA if i % 2 == 0 else PA2
            BB = PB if i % 2 == 0 else PB2
            lh, lht = self.ones2[0:1, :], self.ones2
            b.mm(BA, BA.ap, lh, arow, reads=[lht, self.rowb],
                 start=True, stop=True)
            b.mm(BB, BB.ap, lh, marow, reads=[lht, self.rowb],
                 start=True, stop=True)
            b.op("dve", (lambda i=i, BA=BA: lambda v: v.tensor_tensor(
                out=self.lnscr.ap, in0=xsrc[i][:, sl], in1=BA.ap,
                op=ALU.mult))(), reads=[xsrc[i], BA], writes=[self.lnscr])
            b.op("dve", (lambda i=i, BB=BB: lambda v: v.tensor_tensor(
                out=dst_tiles[i].ap, in0=self.lnscr.ap, in1=BB.ap,
                op=ALU.subtract))(), reads=[self.lnscr, BB],
                writes=[dst_tiles[i]])

    def _gate(self, half):
        b, I = self.b, self.I
        hsl = slice(half * 512, (half + 1) * 512)
        G1, G2 = self.banks[6], self.banks[7]
        for kc in range(NC_F):
            b.mm(G1, G1.t[0:8, :], self.wg1[:, kc * 8:(kc + 1) * 8],
                 self.xth[kc].ap, reads=[self.wg1, self.xth[kc]],
                 start=(kc == 0), stop=(kc == NC_F - 1))
        b.op("act", lambda s: s.activation(
            out=self.score_em[0:8, hsl], in_=G1.t[0:8, :], func=AF.Tanh),
            reads=[G1], writes=[self.score_em])
        b.mm(G2, G2.t[0:8, :], self.wg2.ap, self.score_em[0:8, hsl],
             reads=[self.wg2, self.score_em], start=True, stop=True)
        b.op("dve", lambda v: v.tensor_copy(
            self.lgf.ap, G2.t[0:8, :]), reads=[G2],
            writes=[self.lgf])
        if self.debug and self.stage == 51 and half == 0:
            self._dump("dbg_logits", self.lgf)
        TS = self.banks[6]
        for tc in range(4):
            b.mm(TS, TS.t[0:128, 8 * tc:8 * tc + 8],
                 self.lgf[0:8, tc * 128:(tc + 1) * 128], self.identf.ap,
                 reads=[self.lgf, self.identf],
                 start=(tc == 0), stop=(tc == 3))
        g64, gA, gC, gm4 = self.g64, self.gA, self.gC, self.gm4
        m1, m2 = gm4[:, 0:4], gm4[:, 4:8]
        s1, s2 = gm4[:, 8:12], gm4[:, 12:16]

        def e3(ap):
            return ap.rearrange("p (t e) -> p t e", e=8)

        def b3(ap):
            return ap.rearrange("p (t o) -> p t o", o=1).broadcast_to((128, 4, 8))

        b.op("dve", lambda v: v.tensor_copy(g64.ap, TS.t[0:128, 0:32]),
             reads=[TS], writes=[g64])
        b.op("dve", lambda v: v.tensor_reduce(
            out=m1, in_=e3(g64.ap), axis=AX.X, op=ALU.max, negate=True),
            reads=[g64], writes=[gm4])
        b.op("dve", lambda v: v.tensor_tensor(
            out=e3(gA.ap), in0=e3(g64.ap), in1=b3(m1), op=ALU.add),
            reads=[g64, gm4], writes=[gA])
        b.op("dve", lambda v: v.tensor_single_scalar(
            out=gA.ap, in_=gA.ap, scalar=0.0, op=ALU.is_ge),
            reads=[gA], writes=[gA])
        b.op("dve", lambda v: v.scalar_tensor_tensor(
            out=g64.ap, in0=gA.ap, scalar=-1e30, in1=g64.ap,
            op0=ALU.mult, op1=ALU.add), reads=[gA, g64], writes=[g64])
        b.op("dve", lambda v: v.tensor_reduce(
            out=m2, in_=e3(g64.ap), axis=AX.X, op=ALU.max, negate=True),
            reads=[g64], writes=[gm4])
        b.op("dve", lambda v: v.tensor_tensor(
            out=e3(gC.ap), in0=e3(g64.ap), in1=b3(m2), op=ALU.add),
            reads=[g64, gm4], writes=[gC])
        b.op("dve", lambda v: v.tensor_single_scalar(
            out=gC.ap, in_=gC.ap, scalar=0.0, op=ALU.is_ge),
            reads=[gC], writes=[gC])
        b.op("dve", lambda v: v.tensor_tensor(
            out=s1, in0=m2, in1=m1, op=ALU.subtract),
            reads=[gm4], writes=[gm4])
        b.op("act", lambda s: s.activation(out=s1, in_=s1, func=AF.Sigmoid),
             reads=[gm4], writes=[gm4])
        b.op("dve", lambda v: v.tensor_scalar(
            out=s2, in0=s1, scalar1=-1.0, scalar2=1.0,
            op0=ALU.mult, op1=ALU.add), reads=[gm4], writes=[gm4])
        b.op("dve", lambda v: v.tensor_tensor(
            out=e3(gA.ap), in0=e3(gA.ap), in1=b3(s1), op=ALU.mult),
            reads=[gA, gm4], writes=[gA])
        b.op("dve", lambda v: v.tensor_tensor(
            out=e3(gC.ap), in0=e3(gC.ap), in1=b3(s2), op=ALU.mult),
            reads=[gC, gm4], writes=[gC])
        b.op("dve", lambda v: v.tensor_tensor(
            out=gA.ap, in0=gA.ap, in1=gC.ap, op=ALU.add),
            reads=[gA, gC], writes=[gA])
        b.op("dve", lambda v: v.tensor_copy(self.sbb[0][:, 0:32], gA.ap),
             reads=[gA], writes=[self.sbb[0]])
        G2b = self.banks[7]
        for tc in range(4):
            tsl = slice(half * 512 + tc * 128, half * 512 + (tc + 1) * 128)
            b.mm(G2b, G2b.t[0:8, 0:128],
                 self.sbb[0][:, 8 * tc:8 * tc + 8],
                 self.identb[:, 0:128], reads=[self.sbb[0], self.identb],
                 start=True, stop=True)
            b.op("dve", (lambda tsl=tsl, G2b=G2b: lambda v: v.tensor_copy(
                self.score_em[0:8, tsl], G2b.t[0:8, 0:128]))(),
                reads=[G2b], writes=[self.score_em])

    def _ffn_half(self, half, hp):
        b, I = self.b, self.I
        hsl = slice(half * 512, (half + 1) * 512)
        scoreB = self.rscr[0]
        silu2 = [self.rscr[1], self.lnscr]
        wb = self.wbuf
        WB = 32768
        UPP = [self.banks[6], self.banks[7]]
        SCB = self.banks[5]
        for e in range(NE):
            b.mm(SCB, SCB.ap, self.selb[0:8, e * 128:(e + 1) * 128],
                 self.score_em[0:8, hsl], reads=[self.selb, self.score_em],
                 start=True, stop=True)
            b.op("dve", lambda v: v.tensor_copy(
                scoreB.ap, SCB.ap), reads=[SCB], writes=[scoreB])
            for hc in range(8):
                if hc % 2 == 0 and not (e == 0 and hc == 0):
                    pa = (hc // 2 % 2) * 2
                    pair_ap = self.ab.t[:, WB + pa * 2048:WB + (pa + 2) * 2048]
                    b.dma("pool", pair_ap.rearrange("p (n j) -> p n j", n=2),
                          I["wupT"][e, hc:hc + 2].rearrange("n p j -> p n j"),
                          writes=[wb[pa], wb[pa + 1]])
                buf = wb[(hc // 2 % 2) * 2 + hc % 2]
                bc = self.bupcol[hc % 2]
                b.dma("sync", bc.ap, I["bup"][e, hc], writes=[bc])
                UP = UPP[hc % 2]
                sscr = silu2[hc % 2]
                for kc in range(NC_F):
                    b.mm(UP, UP.ap, buf[:, kc * 128:(kc + 1) * 128],
                         self.xth[kc].ap,
                         reads=[buf, self.xth[kc]] if kc == 0
                         else [self.xth[kc]],
                         start=(kc == 0), stop=(kc == NC_F - 1))
                b.op("act", (lambda bc=bc, UP=UP, sscr=sscr:
                     lambda s: s.activation(
                         out=sscr.ap, in_=UP.ap, func=AF.Silu,
                         bias=bc[:, 0:1]))(), reads=[UP, bc],
                    writes=[sscr])
                b.op("dve", (lambda e=e, hc=hc, sscr=sscr:
                     lambda v: v.tensor_tensor(
                         out=hp[e][hc].ap, in0=sscr.ap,
                         in1=scoreB.ap, op=ALU.mult))(),
                    reads=[sscr, scoreB], writes=[hp[e][hc]])
        for og in range(2):
            DB = self.banks
            b.dma("pool", self.bdn.ap,
                  I["bdown"][0:8, og * 1024:(og + 1) * 1024],
                  writes=[self.bdn])
            for ocl in range(8):
                b.mm(DB[ocl], DB[ocl].ap,
                     self.bdn[0:8, ocl * 128:(ocl + 1) * 128],
                     self.score_em[0:8, hsl],
                     reads=[self.bdn, self.score_em],
                     start=True, stop=False)
            for e in range(NE):
                for hc in range(8):
                    if hc % 4 == 0:
                        pa = (hc // 4 % 2) * 2
                        pair_ap = self.ab.t[:, WB + pa * 2048:
                                            WB + (pa + 2) * 2048]
                        b.dma("pool",
                              pair_ap.rearrange("p (n j) -> p n j", n=4),
                              I["wdT"][e, hc:hc + 4, :,
                                       og * 1024:(og + 1) * 1024]
                              .rearrange("n p j -> p n j"),
                              writes=[wb[pa], wb[pa + 1]])
                    buf = wb[(hc // 4 % 2) * 2 + (hc % 4) // 2]
                    boff = (hc % 2) * 1024
                    for ocl in range(8):
                        last = (e == NE - 1 and hc == 7)
                        b.mm(DB[ocl], DB[ocl].ap,
                             buf[:, boff + ocl * 128:boff + (ocl + 1) * 128],
                             hp[e][hc].ap,
                             reads=[buf, hp[e][hc]] if ocl in (0, 7)
                             else [hp[e][hc]],
                             start=False, stop=last,
                             token=(ocl == 7))
            for ocl in range(8):
                oc = og * 8 + ocl
                b.op("dve", (lambda oc=oc, ocl=ocl, hsl=hsl, DB=DB:
                     lambda v: v.tensor_tensor(
                         out=self.x[oc][:, hsl], in0=self.x[oc][:, hsl],
                         in1=DB[ocl].ap, op=ALU.add))(),
                    reads=[DB[ocl], self.x[oc]], writes=[self.x[oc]])


from concourse.bass_utils import run_bass_kernel_spmd


def kernel(**inputs):
    mk = MK(stage=6, debug=False).build()
    shared = {}
    in_maps = [host_inputs(inputs["L0_attn_x"], inputs, c, shared=shared)
               for c in range(8)]
    res = run_bass_kernel_spmd(mk.nc, in_maps, core_ids=list(range(8)))
    ys = [np.asarray(r["y"]).T for r in res.results]
    return np.concatenate(ys, axis=0).reshape(2, 4096, 2048).astype(np.float32)
